# version 05
# baseline (speedup 1.0000x reference)
import math
import numpy as np
import ml_dtypes
import concourse.bass as bass
import concourse.mybir as mybir
from concourse.bass_utils import run_bass_kernel_spmd

F32 = mybir.dt.float32
BF16 = mybir.dt.bfloat16
AF = mybir.ActivationFunctionType
ALU = mybir.AluOpType
AX = mybir.AxisListType

MAGIC = 12582912.0
EPS = 1e-5
M30 = 30.0
D = 2048
TOK = 1024
TH = 512
NCH = 16
NI = 64
NH = 16
ISQ1 = float(np.float32(1.0) / np.float32(np.sqrt(np.float32(2048.0))))
ISQ2 = float(np.float32(1.0) / np.float32(np.sqrt(np.float32(8192.0))))
S4 = float(np.float32(1.0) / np.float32(np.sqrt(np.sqrt(np.float32(128.0)))))

_DBG = {}
_CACHE = {}


def _hadamard(n):
    H = np.array([[1.0]], np.float32)
    while H.shape[0] < n:
        H = np.block([[H, H], [H, -H]])
    return H.astype(np.float32)


QSEMS = {
    "qs": ["qsA", "wl0", "wl1", "sx0", "sx1", "hl0", "hl1"],
    "qg": ["qgA", "qgR", "qgS", "qgC", "mw0", "mw1", "ti0", "ti1", "to0", "to1",
           "tw0", "tw1", "ks0", "ks1", "vs0", "vs1"],
}


class Plan:
    def __init__(self, active, eng, sems):
        self.active = active
        self.eng = eng
        self.sems = sems

    def flush(self, q):
        for sem in QSEMS[q]:
            self.wait(q, sem, self.val(sem))

    def pub(self, q):
        ins = None
        if self.active == q:
            ins = self.eng[q].nop()
        return self._inc(ins, q, 1)

    def _inc(self, ins, sem, k):
        rec = self.sems[sem]
        rec[1] += k
        if ins is not None:
            ins.then_inc(rec[0], k)
        return rec[1]

    def val(self, sem):
        return self.sems[sem][1]

    def wait(self, who, sem, v):
        if v > 0 and self.active == who:
            self.eng[who].wait_ge(self.sems[sem][0], v)

    def dma(self, q, out, in_, sem, transpose=False):
        ins = None
        if self.active == q:
            ins = self.eng[q].dma_start(out, in_, transpose=transpose)
        return self._inc(ins, sem, 16)

    def mm(self, out, lhsT, rhs, start, stop, sem=None, ldw=True):
        ins = None
        if self.active == "t":
            if ldw:
                self.eng["t"].ldweights(lhsT)
            ins = self.eng["t"].matmul(out, lhsT, rhs, start=start, stop=stop)
        if sem is not None:
            return self._inc(ins, sem, 1)
        return None

    def mmf(self, out, lhsT, rhs, start, stop, sem=None):
        ins = None
        if self.active == "t":
            ins = self.eng["t"].matmul(out, lhsT, rhs, start=start, stop=stop)
        if sem is not None:
            return self._inc(ins, sem, 1)
        return None

    def v(self, meth, *args, sem=None, **kw):
        ins = None
        if self.active == "v":
            ins = getattr(self.eng["v"], meth)(*args, **kw)
        if sem is not None:
            return self._inc(ins, sem, 1)
        return None

    def s(self, meth, *args, sem=None, **kw):
        ins = None
        if self.active == "s":
            ins = getattr(self.eng["s"], meth)(*args, **kw)
        if sem is not None:
            return self._inc(ins, sem, 1)
        return None

    def cc(self, kind, op, groups, ins_, outs_):
        ins = None
        if self.active == "qg":
            ins = self.eng["qg"].collective_compute(
                kind, op, replica_groups=groups, ins=ins_, outs=outs_)
        return self._inc(ins, "cc", 1)

    def barrier(self):
        self.flush("qs")
        self.flush("qg")
        targets = {}
        for e in ("t", "v", "s", "qs", "qg"):
            ins = None
            if self.active == e:
                ins = self.eng[e].nop()
            targets[e] = self._inc(ins, e, 1)
        for e in ("t", "v", "s", "qs", "qg"):
            for w in ("t", "v", "s", "qs", "qg"):
                if w != e:
                    self.wait(e, w, targets[w])



def build_program():
    nc = bass.Bass("TRN2", target_bir_lowering=False,
                   detect_race_conditions=not _DBG.get("noraces", False))
    A = {}

    def din(name, shape, dty, kind="ExternalInput"):
        A[name] = nc.dram_tensor(name, shape, dty, kind=kind).ap()

    din("x0", [D, TOK], F32)
    for n in ("wq", "wk", "wv", "wo"):
        din(n, [D, D], F32)
    din("wg", [8192, D], F32)
    din("wu", [8192, D], F32)
    din("wdn", [D, 8192], F32)
    din("wshard", [3072, D], F32)
    din("bcols", [128, 64], F32)
    din("bvrow", [1, D], F32)
    din("h128c", [128, 128], BF16)
    din("idenbc", [128, 128], BF16)
    din("idenc", [128, 128], F32)
    din("sel4c", [4, TH], F32)
    din("onesc", [128, 128], F32)
    din("m01", [128, 512], BF16)
    din("y", [D, TOK], F32, "ExternalOutput")
    din("cgT", [D, 8192], BF16, "Internal")
    din("cuT", [D, 8192], BF16, "Internal")
    din("cdT", [8192, D], BF16, "Internal")
    din("cwq", [D, D], BF16, "Internal")
    din("cwk", [D, D], BF16, "Internal")
    din("cwv", [D, D], BF16, "Internal")
    din("cwo", [D, D], BF16, "Internal")
    din("ar_in", [1, 16], F32, "Internal")
    din("ar_out", [1, 16], F32, "Internal")
    din("srow", [1, 512], F32, "Internal")
    din("k_dram", [D, TOK], BF16, "Internal")
    din("q_dram", [D, TOK], BF16, "Internal")
    din("v_dram", [TOK, D], BF16, "Internal")
    din("k_all", [2 * D, TOK], BF16, "Internal")
    din("v_all", [2 * TOK, D], BF16, "Internal")
    din("x_spill", [D, TOK], F32, "Internal")
    for name, shape, dty in _DBG.get("outs", []):
        din(name, shape, dty, "ExternalOutput")

    import contextlib
    stk = contextlib.ExitStack()
    with stk:
        sb = lambda *a: stk.enter_context(nc.sbuf_tensor(*a))
        ps = lambda *a: stk.enter_context(nc.psum_tensor(*a))
        sm = lambda n: stk.enter_context(nc.semaphore(n))
        xu = sb("xu", [128, NCH * TOK], F32)
        a2 = sb("a2", [128, NCH * TOK], BF16)
        wst = sb("wst", [128, NCH * TOK], BF16)
        wf32 = sb("wf32", [128, 2 * 1024], F32)
        tb16 = sb("tb16", [128, 2 * 1024], BF16)
        tmp4 = sb("tmp4", [128, 4 * TH], BF16)
        silub = sb("silu", [128, 4 * TH], F32)
        repb = sb("repb", [128, 4 * TH], F32)
        rw = sb("rw", [128, 2048], F32)
        rows = sb("rows", [1, 2048], F32)
        bvr = sb("bvr", [1, 2048], F32)
        quad = sb("quad", [4, 1024], F32)
        sel4 = sb("sel4", [4, TH], F32)
        hest = sb("hest", [128, 128], BF16)
        idenb = sb("idenb", [128, 128], BF16)
        iden = sb("iden", [128, 128], F32)
        onesf = sb("onesf", [128, 128], F32)
        ones16 = sb("ones16", [128, 1], BF16)
        masks = sb("masks", [128, 512], BF16)
        bco = sb("bco", [128, 64], F32)
        bsc = sb("bsc", [128, 64], F32)
        cols = sb("cols", [128, 8], F32)
        p0 = ps("p0", [128, 512], F32)
        p1 = ps("p1", [128, 512], F32)
        p2 = ps("p2", [128, 512], F32)
        p3 = ps("p3", [128, 512], F32)
        pw0 = ps("pw0", [128, 1024], F32)
        pw1 = ps("pw1", [128, 1024], F32)
        semnames = ["t", "v", "s", "qs", "qg", "cc",
                    "qsA", "wl0", "wl1", "sx0", "sx1", "hl0", "hl1",
                    "qgA", "qgR", "qgS", "qgC", "mw0", "mw1", "ti0", "ti1",
                    "to0", "to1", "tw0", "tw1", "ks0", "ks1", "vs0", "vs1"]
        shandles = {n: sm("m_" + n) for n in semnames}
        block = stk.enter_context(nc.Block())
        B = dict(xu=xu, a2=a2, wst=wst, wf32=wf32, tb16=tb16, tmp4=tmp4,
                 silu=silub, repb=repb, rw=rw, rows=rows, bvr=bvr, quad=quad,
                 sel4=sel4, hest=hest, idenb=idenb, iden=iden, onesf=onesf,
                 ones16=ones16,
                 masks=masks, bco=bco, bsc=bsc, cols=cols,
                 p0=p0, p1=p1, p2=p2, p3=p3, pw0=pw0, pw1=pw1)
        def run(active):
            eng = {"t": nc.tensor, "v": nc.vector, "s": nc.scalar,
                   "qs": nc.sync, "qg": nc.gpsimd}
            sems = {k: [v, 0] for k, v in shandles.items()}
            emit_all(Plan(active, eng, sems), B, A)

        @block.tensor
        def _(t):
            run("t")

        @block.vector
        def _(v):
            run("v")

        @block.scalar
        def _(s):
            run("s")

        @block.sync
        def _(s):
            run("qs")

        @block.gpsimd
        def _(g):
            run("qg")

    return nc



def _dump(p, B, A, name, ap):
    if name not in _DBG.get("dump_names", ()):
        return
    p.barrier()
    dv = p.dma("qg", A[name], ap, "qgS")
    p.wait("qg", "qgS", dv)
    p.barrier()


def _dumps(p, B, A, items):
    items = [(n, ap) for (n, ap) in items if n in _DBG.get("dump_names", ())]
    if not items:
        return
    p.barrier()
    dv = 0
    for n, ap in items:
        dv = p.dma("qg", A[n], ap, "qgS")
    p.wait("qg", "qgS", dv)
    p.barrier()


Q_RAW = 0
Q_SCL = 128
Q_T1 = 256
Q_T2 = 384
RG = {0: 512, 1: 768}
CU = {0: 640, 1: 896}


def emit_all(p, B, A):
    st = {"wst0_free": 0, "wst1_free": 0}
    ph = _DBG.get("phases", 4)
    prologue(p, B, A, st)
    if ph >= 1:
        ffn_call(p, B, A, st, first=True)
    if ph >= 2:
        attn_call(p, B, A, st)
    if ph >= 3:
        ffn_call(p, B, A, st, first=False)
    epilogue(p, B, A, st)


def prologue(p, B, A, st):
    rw, rows, cols, onesf = B["rw"], B["rows"], B["cols"], B["onesf"]
    p.dma("qg", B["xu"][:, :].rearrange("q (j t) -> q j t", j=16),
          A["x0"].rearrange("(j q) t -> q j t", q=128), "qgA")
    for name, buf in (("h128c", B["hest"]), ("idenc", B["iden"]),
                      ("idenbc", B["idenb"]),
                      ("sel4c", B["sel4"]), ("onesc", B["onesf"]),
                      ("m01", B["masks"]), ("bcols", B["bco"])):
        p.dma("qs", buf[:, :], A[name], "qsA")
    p.dma("qs", B["bvr"][0:1, :], A["bvrow"], "qsA")
    v_const = p.val("qsA")

    p.v("memset", B["ones16"][:, :], 1.0)
    p.v("memset", cols[:, :], 0.0)
    p.v("memset", B["quad"][0:4, :], 0.0)
    p.v("memset", cols[:, 0:1], -M30)
    p.v("memset", cols[:, 1:2], EPS)
    p.wait("v", "qsA", v_const)
    p.v("tensor_scalar", B["bsc"][:, 0:32], B["bco"][:, 0:32], S4, None, ALU.mult)
    p.v("tensor_copy", B["bsc"][:, 48:64], B["bco"][:, 48:64], sem="v")

    wf32 = B["wf32"]
    v_tile = {0: 0, 1: 0}
    for u in range(48):
        m = u // 16
        row, hh = (u % 16) // 2, u % 2
        buf = u % 2
        r0 = (m * 8 + row) * 128
        p.wait("qg", "v", v_tile[buf])
        vq = p.dma("qg", wf32[:, buf * 1024:(buf + 1) * 1024],
                   A["wshard"][r0:r0 + 128, hh * 1024:(hh + 1) * 1024],
                   "mw0" if buf == 0 else "mw1")
        p.wait("v", "mw0" if buf == 0 else "mw1", vq)
        p.v("reduce_sum", rw[:, 16 + u:17 + u],
            wf32[:, buf * 1024:(buf + 1) * 1024],
            axis=AX.X, apply_absolute_value=True, sem="v")
        v_tile[buf] = p.val("v")
    v_mean = p.val("v")
    p.wait("t", "v", v_mean)
    p.wait("t", "qsA", v_const)
    p.mmf(B["p2"][0:1, 0:48], onesf[:, 0:1], rw[:, 16:64], True, True, sem="t")
    t_mean = p.val("t")
    p.wait("v", "t", t_mean)
    p.v("memset", rows[0:1, 0:16], 0.0)
    for m in range(3):
        p.v("reduce_sum", rows[0:1, m:m + 1], B["p2"][0:1, m * 16:(m + 1) * 16],
            axis=AX.X, sem="v" if m == 2 else None)
    v_mrow = p.val("v")
    p.wait("qg", "v", v_mrow)
    vz = p.dma("qg", A["ar_in"], rows[0:1, 0:16], "qgR")
    p.wait("qg", "qgR", vz)
    p.cc("AllReduce", ALU.add, [[0, 1, 2, 3, 4, 5, 6, 7]],
         [A["ar_in"].opt()], [A["ar_out"].opt()])
    p.wait("qg", "cc", p.val("cc"))
    p.v("memset", rows[0:1, 16:528], 1.0, sem="v")
    v_ms = p.val("v")
    p.wait("qg", "v", v_ms)
    vsum = p.dma("qg", rows[0:1, 16:32], A["ar_out"], "qgR")
    p.wait("v", "qgR", vsum)
    p.v("tensor_scalar", rows[0:1, 16:144], rows[0:1, 16:144],
        1.0 / (8192.0 * 2048.0), EPS, ALU.mult, ALU.max)
    p.v("reciprocal", rows[0:1, 144:272], rows[0:1, 16:144])
    p.v("tensor_scalar", rows[0:1, 272:400], rows[0:1, 16:144], 1.0 / 127.0, None,
        ALU.mult)
    p.v("tensor_scalar", rows[0:1, 400:528], rows[0:1, 16:144], 0.5 / 127.0, None,
        ALU.mult, sem="v")
    v_srow = p.val("v")
    p.wait("qg", "v", v_srow)
    vr = p.dma("qg", A["srow"], rows[0:1, 0:512], "qgR")
    p.wait("qg", "qgR", vr)
    p.dma("qg", cols[:, 2:5], A["srow"][0:1, 144:147].partition_broadcast(128), "qgR")
    p.dma("qg", cols[:, 5:7], A["srow"][0:1, 272:274].partition_broadcast(128), "qgR")
    p.dma("qg", cols[:, 7:8], A["srow"][0:1, 402:403].partition_broadcast(128), "qgR")
    _dump(p, B, A, "d_rows", rows[0:1, :])
    _dump(p, B, A, "d_ar", A["ar_out"])
    st["cast_done"] = 0
    p.barrier()



def _ln(p, B, xsl, dst):
    rows, rw, onesf = B["rows"], B["rw"], B["onesf"]
    p2, p3, pw0 = B["p2"], B["p3"], B["pw0"]
    p.wait("t", "v", p.val("v"))
    p.wait("t", "s", p.val("s"))
    t_sq = {0: 0, 1: 0}
    for j in range(NCH):
        buf = j % 2
        p.wait("s", "t", t_sq[buf])
        p.s("activation", rw[:, 1024 + buf * 512:1024 + buf * 512 + 512], xsl(j),
            AF.Square, sem="s")
        sj = p.val("s")
        p.mmf(p2[0:1, 0:TH], onesf[:, 0:1], xsl(j), j == 0, j == NCH - 1)
        p.wait("t", "s", sj)
        p.mmf(p3[0:1, 0:TH], onesf[:, 0:1],
              rw[:, 1024 + buf * 512:1024 + buf * 512 + 512], j == 0, j == NCH - 1,
              sem="t")
        t_sq[buf] = p.val("t")
    _pad(p, B)
    t_stats = p.val("t")
    p.wait("v", "t", t_stats)
    R_M, R_MSQ, R_VAR, R_R = 0, 512, 1024, 1536
    p.v("tensor_scalar", rows[0:1, R_M:R_M + TH], p2[0:1, 0:TH], 1.0 / D, None,
        ALU.mult)
    p.v("tensor_scalar", rows[0:1, R_MSQ:R_MSQ + TH], p3[0:1, 0:TH], 1.0 / D, None,
        ALU.mult)
    p.v("tensor_mul", rows[0:1, R_VAR:R_VAR + TH], rows[0:1, R_M:R_M + TH],
        rows[0:1, R_M:R_M + TH])
    p.v("tensor_sub", rows[0:1, R_VAR:R_VAR + TH], rows[0:1, R_MSQ:R_MSQ + TH],
        rows[0:1, R_VAR:R_VAR + TH], sem="v")
    v_var = p.val("v")
    p.wait("s", "v", v_var)
    p.s("activation", rows[0:1, R_MSQ:R_MSQ + TH], rows[0:1, R_VAR:R_VAR + TH],
        AF.Sqrt, bias=B["cols"][0:1, 1:2], sem="s")
    s_sd = p.val("s")
    p.wait("v", "s", s_sd)
    p.v("reciprocal", rows[0:1, R_R:R_R + TH], rows[0:1, R_MSQ:R_MSQ + TH], sem="v")
    v_rows = p.val("v")
    p.wait("t", "v", v_rows)
    p.wait("t", "s", p.val("s"))
    p.mmf(pw0[:, 0:512], onesf[0:1, :], rows[0:1, R_R:R_R + TH], True, True)
    p.mmf(pw0[:, 512:1024], onesf[0:1, :], rows[0:1, R_M:R_M + TH], True, True)
    _pad(p, B)
    t_rep = p.val("t")
    p.wait("s", "t", t_rep)
    p.s("copy", B["repb"][:, 0:512], pw0[:, 0:512])
    p.s("copy", B["repb"][:, 512:1024], pw0[:, 512:1024], sem="s")
    s_rep = p.val("s")
    p.wait("v", "s", s_rep)
    for j in range(NCH):
        buf = j % 2
        p.v("tensor_sub", rw[:, buf * 512:buf * 512 + 512], xsl(j),
            B["repb"][:, 512:1024])
        p.v("tensor_mul", dst(j), rw[:, buf * 512:buf * 512 + 512],
            B["repb"][:, 0:512], sem="v")
    return p.val("v")


def _radix4(p, buf_of, nch, width, tmp):
    levels = {16: 2, 64: 3}[nch]
    for lv in range(levels):
        s = 4 ** lv
        for base in range(nch):
            if (base // s) % 4 != 0:
                continue
            c0, c1, c2, c3 = (buf_of(base), buf_of(base + s),
                              buf_of(base + 2 * s), buf_of(base + 3 * s))
            T = [tmp[:, i * width:(i + 1) * width] for i in range(4)]
            p.v("tensor_add", T[0], c0, c1)
            p.v("tensor_sub", T[1], c0, c1)
            p.v("tensor_add", T[2], c2, c3)
            p.v("tensor_sub", T[3], c2, c3)
            p.v("tensor_add", c0, T[0], T[2])
            p.v("tensor_add", c1, T[1], T[3])
            p.v("tensor_sub", c2, T[0], T[2])
            p.v("tensor_sub", c3, T[1], T[3], sem="v")
    return p.val("v")


def _quant(p, B, src_of, nchunks, scale_pre, dst_of, cb):
    rw, quad, iden, sel4 = B["rw"], B["quad"], B["iden"], B["sel4"]
    hest = B["hest"]
    p0, p1, pw1 = B["p0"], B["p1"], B["pw1"]
    pf_free = {0: p.val("v"), 1: p.val("v")}
    for c in range(nchunks):
        buf = c % 2
        pf = (p0, p1)[buf]
        p.wait("t", "v", pf_free[buf])
        tv = p.mm(pf[:, :], hest[:, :], src_of(c), True, True, sem="t",
                  ldw=(c == 0))
        p.wait("v", "t", tv)
        if c == 0:
            p.v("tensor_copy", rw[:, 0:512], pf[:, :])
            p.v("tensor_copy", rw[:, 512:1024], pf[:, :], sem="v")
        else:
            p.v("tensor_tensor", rw[:, 0:512], rw[:, 0:512], pf[:, :], ALU.max)
            p.v("tensor_tensor", rw[:, 512:1024], rw[:, 512:1024], pf[:, :],
                ALU.min, sem="v")
        pf_free[buf] = p.val("v")
    _vgap(p, B, 1)
    p.v("tensor_scalar", rw[:, 512:1024], rw[:, 512:1024], -1.0, None, ALU.mult)
    p.v("tensor_tensor", rw[:, 1024:1536], rw[:, 0:512], rw[:, 512:1024], ALU.max,
        sem="v")
    v_mm = p.val("v")
    p.wait("t", "v", v_mm)
    for b in range(4):
        p.mmf(B["p2"][:, b * 128:(b + 1) * 128],
              rw[:, 1024 + b * 128:1024 + (b + 1) * 128], iden[:, :], True, True)
    _pad(p, B)
    t_tp = p.val("t")
    p.wait("v", "t", t_tp)
    for b in range(4):
        p.v("reduce_max", rw[:, 1536 + b:1537 + b], B["p2"][:, b * 128:(b + 1) * 128],
            axis=AX.X, sem="v" if b == 3 else None)
    v_ct = p.val("v")
    p.wait("t", "v", v_ct)
    p.mmf(B["p3"][0:4, 0:128], rw[:, 1536:1540], iden[:, :], True, True)
    _pad(p, B)
    t_qr = p.val("t")
    p.wait("v", "t", t_qr)
    p.v("tensor_scalar", quad[0:4, Q_RAW:Q_RAW + 128], B["p3"][0:4, 0:128],
        scale_pre, None, ALU.mult, sem="v")
    cb()
    v_sc = p.v("engine_nop", sem="v")
    p.wait("t", "v", v_sc)
    p.wait("t", "s", p.val("s"))
    for b in range(4):
        p.mmf(pw1[:, b * 128:(b + 1) * 128], sel4[:, b * 128:(b + 1) * 128],
              quad[0:4, Q_SCL:Q_SCL + 128], True, True)
    _pad(p, B)
    t_rp = p.val("t")
    p.wait("s", "t", t_rp)
    p.s("copy", B["repb"][:, 2 * 512:3 * 512], pw1[:, 0:512], sem="s")
    s_rp = p.val("s")
    p.wait("v", "s", s_rp)
    vv = {0: p.val("v"), 1: p.val("v")}
    for c in range(nchunks):
        buf = c % 2
        pf = (p0, p1)[buf]
        p.wait("t", "v", vv[buf])
        tv = p.mm(pf[:, :], hest[:, :], src_of(c), True, True, sem="t",
                  ldw=(c == 0))
        p.wait("v", "t", tv)
        p.v("tensor_mul", rw[:, buf * 512:buf * 512 + 512], pf[:, :],
            B["repb"][:, 2 * 512:3 * 512], sem="v")
        vv[buf] = p.val("v")
        if c > 0:
            pb = (c - 1) % 2
            p.v("tensor_scalar", dst_of(c - 1), rw[:, pb * 512:pb * 512 + 512],
                MAGIC, MAGIC, ALU.add, ALU.subtract, sem="v")
    _vgap(p, B, 1)
    lb = (nchunks - 1) % 2
    p.v("tensor_scalar", dst_of(nchunks - 1), rw[:, lb * 512:lb * 512 + 512],
        MAGIC, MAGIC, ALU.add, ALU.subtract, sem="v")
    return p.val("v")


def _pad(p, B):
    return p.mmf(B["p0"][:, 0:128], B["onesf"][0:1, :], B["rows"][0:1, 0:128],
                 True, True, sem="t")


def _vgap(p, B, n=1):
    for _ in range(n):
        p.v("tensor_scalar", B["rows"][0:1, 512:1536], B["rows"][0:1, 512:1536],
            1.0, None, ALU.mult)


def _repl_quad(p, B, qoff, dst_off):
    quad, sel4, pw1 = B["quad"], B["sel4"], B["pw1"]
    p.wait("t", "v", p.val("v"))
    p.wait("t", "s", p.val("s"))
    for b in range(4):
        p.mmf(pw1[:, b * 128:(b + 1) * 128], sel4[:, b * 128:(b + 1) * 128],
              quad[0:4, qoff:qoff + 128], True, True)
    _pad(p, B)
    t_r = p.val("t")
    p.wait("s", "t", t_r)
    p.s("copy", B["repb"][:, dst_off:dst_off + 512], pw1[:, 0:512], sem="s")
    s_r = p.val("s")
    p.wait("v", "s", s_r)
    return s_r


def _wload_blockT(p, B, A, cacheT, r0, c0, buf):
    wst = B["wst"]
    sem = "wl0" if buf == 0 else "wl1"
    dst = wst[:, buf * 8192:(buf + 1) * 8192].rearrange("q (j t) -> q j t", j=16)
    srcv = cacheT[r0:r0 + 2048, c0:c0 + 512].rearrange("(j q) t -> q j t", q=128)
    v = p.dma("qs", dst, srcv, sem)
    return sem, v


def _wload_block(p, B, A, cache, r0, r1, buf):
    wst = B["wst"]
    sem = "wl0" if buf == 0 else "wl1"
    v = 0
    for h in range(16):
        dst = wst[:, buf * 8192 + h * 512: buf * 8192 + h * 512 + (r1 - r0)]
        v = p.dma("qs", dst, cache[r0:r1, h * 128:(h + 1) * 128], sem,
                  transpose=True)
    return sem, v


def _wload_attnT(p, B, A, st, cache, og, pwslices, wst_free):
    wst, idenb = B["wst"], B["idenb"]
    wfb = B["wf32"][:, :].bitcast(BF16)
    p.wait("v", "t", wst_free)
    evs = []
    for t4 in range(4):
        b2 = t4 % 2
        lsem = "wl0" if b2 == 0 else "wl1"
        p.wait("qs", "v", st.get(("awl_evac", b2), 0))
        lv = p.dma("qs", wfb[:, b2 * 2048:(b2 + 1) * 2048],
                   cache[og * 512 + t4 * 128: og * 512 + (t4 + 1) * 128, :],
                   lsem)
        p.wait("t", lsem, lv)
        p.wait("t", "v", st.get("awl_pfree", 0))
        for j in range(16):
            ps = pwslices[j // 4]
            p.mm(ps[:, (j % 4) * 128:(j % 4 + 1) * 128],
                 wfb[:, b2 * 2048 + j * 128: b2 * 2048 + (j + 1) * 128],
                 idenb[:, :], True, True, sem="t" if j == 15 else None)
        tv = p.val("t")
        p.wait("v", "t", tv)
        for k in range(4):
            p.v("tensor_copy",
                wst[:, t4 * 2048 + k * 512: t4 * 2048 + (k + 1) * 512],
                pwslices[k][:, 0:512], sem="v" if k == 3 else None)
        st[("awl_evac", b2)] = p.val("v")
        st["awl_pfree"] = p.val("v")
        evs.append(p.val("v"))
    return evs


def _tern_pair(p, B, A, st, wsrc, cdstT, otile, hhalf, icol):
    wf32, tb16, cols, rw = B["wf32"], B["tb16"], B["cols"], B["rw"]
    idenb = B["idenb"]
    pw0, pw1 = B["pw0"], B["pw1"]
    rwb = rw[:, :].bitcast(BF16)
    p.wait("qg", "v", st["tern_vdone"])
    vq = p.dma("qg", wf32[:, :].rearrange("q (b t) -> q b t", b=2),
               wsrc[otile * 128:(otile + 2) * 128,
                    hhalf * 1024:(hhalf + 1) * 1024]
               .rearrange("(b q) t -> q b t", q=128), "ti0")
    p.wait("v", "ti0", vq)
    p.wait("v", "t", st["tern_tdone"])
    sl = wf32[:, :]
    ol = tb16[:, :]
    p.v("tensor_scalar", sl, sl, cols[:, icol:icol + 1], 1.49, ALU.mult, ALU.min)
    p.v("tensor_scalar", sl, sl, -1.49, MAGIC, ALU.max, ALU.add)
    p.v("tensor_scalar", ol, sl, MAGIC, None, ALU.subtract, sem="v")
    st["tern_vdone"] = p.val("v")
    p.wait("t", "v", st["tern_vdone"])
    p.wait("t", "v", st["tern_pevac"])
    for b in range(2):
        for j in range(8):
            pw = pw0 if j < 4 else pw1
            c0 = (j % 4) * 256 + b * 128
            p.mm(pw[:, c0:c0 + 128], ol[:, b * 1024 + j * 128:
                                         b * 1024 + (j + 1) * 128],
                 idenb[:, :], True, True,
                 sem="t" if (b == 1 and j == 7) else None)
    st["tern_tdone"] = p.val("t")
    p.wait("v", "t", st["tern_tdone"])
    p.wait("v", "to0", st["tern_sdone"])
    p.v("tensor_copy", rwb[:, 2048:3072], pw0[:, 0:1024])
    p.v("tensor_copy", rwb[:, 3072:4096], pw1[:, 0:1024], sem="v")
    st["tern_pevac"] = p.val("v")
    p.wait("qg", "v", st["tern_pevac"])
    base = otile * 128
    dst = cdstT[hhalf * 1024:(hhalf + 1) * 1024, base:base + 256]
    od = p.dma("qg", dst.rearrange("(j q) c -> q j c", q=128),
               rwb[:, 2048:4096].rearrange("q (j c) -> q j c", j=8), "to0")
    st["tern_sdone"] = od
    return od


def _tern_og(p, B, A, st, which, og):
    wsrc = A["wg"] if which == "g" else A["wu"]
    cdstT = A["cgT"] if which == "g" else A["cuT"]
    icol = 2 if which == "g" else 3
    for hh in range(2):
        for otp in range(2):
            _tern_pair(p, B, A, st, wsrc, cdstT, og * 4 + otp * 2, hh, icol)
    p.wait("qg", "to0", st["tern_sdone"])
    st["cg_prog" if which == "g" else "cu_prog"].append(p.pub("qg"))


def ffn_call(p, B, A, st, first):
    xu, a2, wst = B["xu"], B["a2"], B["wst"]
    rw, quad, cols, silub = B["rw"], B["quad"], B["cols"], B["silu"]
    p0, p1, p2, p3 = B["p0"], B["p1"], B["p2"], B["p3"]
    ub = xu[:, :].bitcast(BF16)

    if first:
        st["tern_vdone"] = 0
        st["tern_tdone"] = 0
        st["tern_pevac"] = 0
        st["tern_sdone"] = 0
        st["cg_prog"] = []
        st["cu_prog"] = []

    def xsl(j, half):
        return xu[:, j * TOK + half * TH: j * TOK + half * TH + TH]

    def q1(j, half):
        return a2[:, half * 8192 + j * TH: half * 8192 + (j + 1) * TH]

    for half in range(2):
        _ln(p, B, lambda j, h=half: xsl(j, h), lambda j, h=half: q1(j, h))
        _radix4(p, lambda j, h=half: q1(j, h), 16, TH, B["tmp4"])

        def q1_cb(h=half):
            p.v("tensor_scalar", quad[0:4, Q_T1:Q_T1 + 128],
                quad[0:4, Q_RAW:Q_RAW + 128], 1.0, EPS, ALU.mult, ALU.max)
            p.v("tensor_scalar", quad[0:4, Q_T2:Q_T2 + 128],
                quad[0:4, Q_T1:Q_T1 + 128], 1.0 / (127.0 * ISQ1), None, ALU.mult)
            p.v("tensor_scalar", quad[0:4, RG[h]:RG[h] + 128],
                quad[0:4, Q_T1:Q_T1 + 128], cols[0:4, 5:6], None, ALU.mult)
            p.v("tensor_scalar", quad[0:4, CU[h]:CU[h] + 128],
                quad[0:4, Q_T1:Q_T1 + 128], cols[0:4, 6:7], None, ALU.mult)
            p.v("reciprocal", quad[0:4, Q_SCL:Q_SCL + 128],
                quad[0:4, Q_T2:Q_T2 + 128], sem="v")
            _vgap(p, B, 3)

        if _DBG.get("split_q1") and half == 0:
            _quant(p, B, lambda j, h=half: q1(j, h), 16, ISQ1,
                   lambda j: a2[:, 8192 + j * TH: 8192 + (j + 1) * TH], q1_cb)
        else:
            _quant(p, B, lambda j, h=half: q1(j, h), 16, ISQ1,
                   lambda j, h=half: q1(j, h), q1_cb)
        if half == 0:
            _dumps(p, B, A, [
                ("d_an", a2[:, 0:8192]),
                ("d_q1h0", a2[:, 8192:16384] if _DBG.get("split_q1")
                 else a2[:, 0:8192]),
                ("d_repb0", B["repb"][:, :]),
                ("d_rw0", B["rw"][:, :]),
                ("d_quad0", B["quad"][0:4, :]),
            ])

    _dumps(p, B, A, [
        ("d_q1", a2[:, :]),
        ("d_repb", B["repb"][:, :]),
        ("d_quad", B["quad"][0:4, :]),
        ("d_cols", cols[:, :]),
    ])
    v_fin = p.val("v")
    p.wait("qg", "v", v_fin)
    spill_v = p.dma("qg", A["x_spill"].rearrange("(j q) t -> q j t", q=128),
                    xu[:, :].rearrange("q (j t) -> q j t", j=16), "qgS")
    st["spill_v"] = spill_v

    if first and not st["cg_prog"]:
        _tern_og(p, B, A, st, "g", 0)
        _tern_og(p, B, A, st, "u", 0)

    for half in range(2):
        hoff = half * TH
        p.wait("t", "to0", st.get("tern_sdone", 0))
        _repl_quad(p, B, RG[half], 3 * 512)

        def uch(c):
            return ub[:, c * TH:(c + 1) * TH]

        for i2 in range(2):
            st[("pgf", i2)] = st[("puf", i2)] = p.val("v")
        first_u_guard = True
        for og in range(16):
            if first:
                while len(st["cg_prog"]) <= og:
                    _tern_og(p, B, A, st, "g", len(st["cg_prog"]))
                    _tern_og(p, B, A, st, "u", len(st["cu_prog"]))
                p.wait("qs", "qg", st["cg_prog"][og])
            p.wait("qs", "t", st["wst0_free"])
            g_sem, g_val = _wload_blockT(p, B, A, A["cgT"], 0, og * 512, 0)
            for oc in range(4):
                pg = (p0, p1)[oc % 2]
                p.wait("t", "v", st.get(("pgf", oc % 2), 0))
                p.wait("t", g_sem, g_val)
                for h in range(16):
                    lhs = wst[:, h * 512 + oc * 128: h * 512 + oc * 128 + 128]
                    p.mm(pg[:, :], lhs, q1(h, half), h == 0, h == 15,
                         sem="t" if h == 15 else None)
                t_g = p.val("t")
                p.wait("v", "t", t_g)
                p.wait("v", "s", st.get(("rwg", oc % 2), 0))
                p.v("tensor_mul", rw[:, (oc % 2) * 512:(oc % 2) * 512 + 512],
                    pg[:, :], B["repb"][:, 3 * 512:4 * 512], sem="v")
                st[("pgf", oc % 2)] = p.val("v")
                p.wait("s", "v", p.val("v"))
                p.wait("s", "v", st.get(("silf", oc), 0))
                p.s("activation", silub[:, oc * 512:(oc + 1) * 512],
                    rw[:, (oc % 2) * 512:(oc % 2) * 512 + 512], AF.Silu, sem="s")
                st[("rwg", oc % 2)] = p.val("s")
                st[("sil_rdy", oc)] = p.val("s")
            st["wst0_free"] = p.val("t")
            if first:
                p.wait("qs", "qg", st["cu_prog"][og])
            p.wait("qs", "t", st["wst1_free"])
            u_sem, u_val = _wload_blockT(p, B, A, A["cuT"], 0, og * 512, 1)
            for oc in range(4):
                pu = (p2, p3)[oc % 2]
                p.wait("t", "v", st.get(("puf", oc % 2), 0))
                p.wait("t", u_sem, u_val)
                for h in range(16):
                    lhs = wst[:, 8192 + h * 512 + oc * 128:
                              8192 + h * 512 + oc * 128 + 128]
                    p.mm(pu[:, :], lhs, q1(h, half), h == 0, h == 15,
                         sem="t" if h == 15 else None)
                t_u = p.val("t")
                p.wait("v", "t", t_u)
                p.wait("v", "s", st.get(("sil_rdy", oc), 0))
                if first_u_guard:
                    p.wait("v", "qgS", st["spill_v"])
                    first_u_guard = False
                p.v("tensor_mul", uch(og * 4 + oc),
                    silub[:, oc * 512:(oc + 1) * 512], pu[:, :], sem="v")
                st[("puf", oc % 2)] = p.val("v")
                st[("silf", oc)] = p.val("v")
            st["wst1_free"] = p.val("t")
            if first and half == 0:
                if len(st["cg_prog"]) <= og + 1 and og + 1 < 16:
                    _tern_og(p, B, A, st, "g", og + 1)
                    _tern_og(p, B, A, st, "u", og + 1)
                for k2 in range(4):
                    idx = og * 4 + k2
                    hh2, otp = idx // 8, idx % 8
                    _tern_pair(p, B, A, st, A["wdn"], A["cdT"], 2 * otp, hh2, 4)
                if og == 15:
                    p.wait("qg", "to0", st["tern_sdone"])
                    st["cd_done"] = p.pub("qg")

        if half == 0:
            _dump(p, B, A, "d_u", xu[:, :].bitcast(BF16))

        _radix4(p, uch, 64, TH, B["tmp4"])

        if half == 0:
            _dump(p, B, A, "d_uh", xu[:, :].bitcast(BF16))

        def q2_cb(h=half):
            p.v("tensor_mul", quad[0:4, Q_T1:Q_T1 + 128],
                quad[0:4, Q_RAW:Q_RAW + 128], quad[0:4, CU[h]:CU[h] + 128])
            _vgap(p, B, 1)
            p.v("tensor_scalar", quad[0:4, Q_T1:Q_T1 + 128],
                quad[0:4, Q_T1:Q_T1 + 128], 1.0, EPS, ALU.mult, ALU.max)
            p.v("tensor_scalar", quad[0:4, Q_T2:Q_T2 + 128],
                quad[0:4, Q_T1:Q_T1 + 128], cols[0:4, 7:8], None, ALU.mult)
            p.v("tensor_scalar", quad[0:4, Q_RAW:Q_RAW + 128],
                quad[0:4, Q_T1:Q_T1 + 128], 1.0 / (127.0 * ISQ2), None, ALU.mult)
            p.v("reciprocal", quad[0:4, Q_T1:Q_T1 + 128],
                quad[0:4, Q_RAW:Q_RAW + 128], sem="v")
            _vgap(p, B, 3)
            p.v("tensor_mul", quad[0:4, Q_SCL:Q_SCL + 128],
                quad[0:4, Q_T1:Q_T1 + 128], quad[0:4, CU[h]:CU[h] + 128], sem="v")
            _vgap(p, B, 1)

        p.wait("t", "to0", st.get("tern_sdone", 0))
        _quant(p, B, uch, 64, ISQ2, uch, q2_cb)
        if half == 0:
            _dump(p, B, A, "d_q2", xu[:, :].bitcast(BF16))
            _dump(p, B, A, "d_quad2", B["quad"][0:4, :])
        _repl_quad(p, B, Q_T2, 3 * 512)

        for i2 in range(4):
            st[("pdf", i2)] = p.val("v")
        for odg in range(4):
            for i in range(4):
                p.wait("t", "v", st.get(("pdf", i), 0))
            kq_t = {0: st["wst0_free"], 1: st["wst1_free"]}
            for kq in range(4):
                buf = kq % 2
                p.wait("qs", "t", kq_t[buf])
                if first:
                    p.wait("qs", "qg", st["cd_done"])
                dsem, d_val = _wload_blockT(p, B, A, A["cdT"], kq * 2048,
                                            odg * 512, buf)
                p.wait("t", dsem, d_val)
                for oc in range(4):
                    pd = (p0, p1, p2, p3)[oc]
                    for h in range(16):
                        h2 = kq * 16 + h
                        lhs = wst[:, buf * 8192 + h * 512 + oc * 128:
                                  buf * 8192 + h * 512 + oc * 128 + 128]
                        p.mm(pd[:, :], lhs, uch(h2), (kq == 0) and (h == 0),
                             (kq == 3) and (h == 15),
                             sem="t" if h == 15 else None)
                kq_t[buf] = p.val("t")
            st["wst0_free"] = kq_t[0]
            st["wst1_free"] = kq_t[1]
            t_dn = p.val("t")
            for oc in range(4):
                pd = (p0, p1, p2, p3)[oc]
                j = odg * 4 + oc
                sx = rw[:, 1024 + (oc % 2) * 512: 1024 + (oc % 2) * 512 + 512]
                t1 = rw[:, (oc % 2) * 512:(oc % 2) * 512 + 512]
                sxs = "sx0" if oc % 2 == 0 else "sx1"
                tws = "tw0" if oc % 2 == 0 else "tw1"
                p.wait("qs", "v", st.get(("sxf", oc % 2), 0))
                vsx = p.dma("qs", sx, A["x_spill"][j * 128:(j + 1) * 128,
                                                   hoff:hoff + TH], sxs)
                p.wait("v", "t", t_dn)
                p.wait("v", sxs, vsx)
                p.wait("v", tws, st.get(("t1f", oc % 2), 0))
                p.v("tensor_mul", t1, pd[:, :], B["repb"][:, 3 * 512:4 * 512])
                p.v("tensor_add", t1, t1, sx, sem="v")
                st[("pdf", oc)] = p.val("v")
                st[("sxf", oc % 2)] = p.val("v")
                v_t1 = p.val("v")
                p.wait("qg", "v", v_t1)
                st[("t1f", oc % 2)] = p.dma(
                    "qg", A["x_spill"][j * 128:(j + 1) * 128, hoff:hoff + TH], t1,
                    tws)
        if first and half == 0:
            for wsrc, wdst in (("wq", "cwq"), ("wk", "cwk"), ("wv", "cwv"),
                               ("wo", "cwo")):
                st["cast_done"] = p.dma("qg", A[wdst], A[wsrc], "qgC")

    p.flush("qg")
    p.wait("qg", "t", p.val("t"))
    rst = p.dma("qg", xu[:, :].rearrange("q (j t) -> q j t", j=16),
                A["x_spill"].rearrange("(j q) t -> q j t", q=128), "qgS")
    p.wait("qg", "qgS", rst)
    p.barrier()


def attn_call(p, B, A, st):
    xu, a2, wst = B["xu"], B["a2"], B["wst"]
    rw, rows, cols = B["rw"], B["rows"], B["cols"]
    p0, p1, p2, p3, pw0, pw1 = B["p0"], B["p1"], B["p2"], B["p3"], B["pw0"], B["pw1"]
    masks, bsc, tb16 = B["masks"], B["bsc"], B["tb16"]

    def anf(j):
        return a2[:, j * TOK:(j + 1) * TOK]

    for half in range(2):
        _ln(p, B,
            lambda j, h=half: xu[:, j * TOK + h * TH: j * TOK + h * TH + TH],
            lambda j, h=half: a2[:, j * TOK + h * TH: j * TOK + h * TH + TH])

    p.wait("qs", "qgC", st["cast_done"])
    p.wait("t", "v", p.val("v"))
    st[("pqf", 0)] = st[("pqf", 1)] = p.val("s")

    for wc, bcb, ddst in (("cwq", 0, "q_dram"), ("cwk", 16, "k_dram")):
        for og in range(4):
            evs = _wload_attnT(p, B, A, st, A[wc], og,
                               [B["p0"], B["p1"], B["p2"], B["p3"]],
                               st["wst0_free"])
            for oc in range(4):
                o = og * 4 + oc
                pq = (pw0, pw1)[oc % 2]
                p.wait("t", "s", st.get(("pqf", oc % 2), 0))
                p.wait("t", "v", evs[oc])
                for h in range(16):
                    lhs = wst[:, oc * 2048 + h * 128: oc * 2048 + h * 128 + 128]
                    for qh in range(2):
                        p.mm(pq[:, qh * 512:(qh + 1) * 512], lhs,
                             anf(h)[:, qh * 512:(qh + 1) * 512], h == 0, h == 15,
                             sem="t" if (h == 15 and qh == 1) else None,
                             ldw=(qh == 0))
                t_q = p.val("t")
                p.wait("s", "t", t_q)
                ks = tb16[:, (oc % 2) * 1024:(oc % 2) * 1024 + 1024]
                kss = "ks0" if oc % 2 == 0 else "ks1"
                p.wait("s", kss, st.get(("ksf", oc % 2), 0))
                p.s("activation", ks, pq[:, :], AF.Identity,
                    bias=bsc[:, bcb + o:bcb + o + 1], scale=S4, sem="s")
                s_k = p.val("s")
                p.wait("qg", "s", s_k)
                st[("ksf", oc % 2)] = p.dma(
                    "qg", A[ddst][o * 128:(o + 1) * 128, :], ks, kss)
                st[("pqf", oc % 2)] = p.val("s")
            st["wst0_free"] = p.val("t")

    st["vs_init"] = (p.val("ks0"), p.val("ks1"))
    for dq in range(4):
        p.wait("t", "v", p.val("v"))
        p.wait("t", "s", p.val("s"))
        p.mmf(pw0[:, 0:512], B["onesf"][0:1, :], B["bvr"][0:1, dq * 512:(dq + 1) * 512],
              True, True)
        _pad(p, B)
        t_bv = p.val("t")
        p.wait("s", "t", t_bv)
        p.s("copy", B["repb"][:, 2 * 512:3 * 512], pw0[:, 0:512], sem="s")
        s_bv = p.val("s")
        p.wait("v", "s", s_bv)
        p.wait("qs", "t", st["wst0_free"])
        lsem, lval = _wload_block(p, B, A, A["cwv"], dq * 512, dq * 512 + 512, 0)
        for tb in range(8):
            pv = (p0, p1, p2, p3)[tb % 4]
            p.wait("t", "v", st.get(("pvf", tb % 4), 0))
            p.wait("t", lsem, lval)
            for h in range(16):
                lhs = anf(h)[:, tb * 128:(tb + 1) * 128]
                rhsw = wst[:, h * 512:(h + 1) * 512]
                p.mm(pv[:, :], lhs, rhsw, h == 0, h == 15,
                     sem="t" if h == 15 else None)
            t_v = p.val("t")
            p.wait("v", "t", t_v)
            vs = tb16[:, (tb % 2) * 1024:(tb % 2) * 1024 + 512]
            vss = "vs0" if tb % 2 == 0 else "vs1"
            if dq == 0 and tb < 2:
                p.wait("v", "ks0", st["vs_init"][0])
                p.wait("v", "ks1", st["vs_init"][1])
            p.wait("v", vss, st.get(("vsf", tb % 2), 0))
            p.v("tensor_add", vs, pv[:, :], B["repb"][:, 2 * 512:3 * 512], sem="v")
            st[("pvf", tb % 4)] = p.val("v")
            v_vs = p.val("v")
            p.wait("qg", "v", v_vs)
            st[("vsf", tb % 2)] = p.dma(
                "qg", A["v_dram"][tb * 128:(tb + 1) * 128, dq * 512:(dq + 1) * 512],
                vs, vss)
        st["wst0_free"] = p.val("t")

    p.flush("qg")
    p.cc("AllGather", ALU.bypass, [[0, 1], [2, 3], [4, 5], [6, 7]],
         [A["k_dram"].opt()], [A["k_all"].opt()])
    p.cc("AllGather", ALU.bypass, [[0, 1], [2, 3], [4, 5], [6, 7]],
         [A["v_dram"].opt()], [A["v_all"].opt()])
    cc_done = p.val("cc")
    p.wait("qs", "cc", cc_done)
    p.wait("t", "v", p.val("v"))
    p.wait("t", "s", p.val("s"))

    HB = 5120

    def kat(b):
        return wst[:, b * HB: b * HB + 2048]

    def vat(b):
        return wst[:, b * HB + 2048: b * HB + 4096]

    def qat(b):
        return wst[:, b * HB + 4096: b * HB + 5120]

    def pat(i):
        return wst[:, 10240 + (i % 3) * 1024: 10240 + (i % 3) * 1024 + 1024]

    def load_head(h, b):
        sem = "hl0" if b == 0 else "hl1"
        p.dma("qs", kat(b)[:, 0:1024], A["k_all"][h * 128:(h + 1) * 128, :], sem)
        p.dma("qs", kat(b)[:, 1024:2048],
              A["k_all"][2048 + h * 128: 2048 + (h + 1) * 128, :], sem)
        p.dma("qs", vat(b).rearrange("q (kb d) -> q kb d", kb=16),
              A["v_all"][:, h * 128:(h + 1) * 128]
              .rearrange("(kb q) d -> q kb d", q=128), sem)
        v = p.dma("qs", qat(b), A["q_dram"][h * 128:(h + 1) * 128, :], sem)
        return sem, v

    p.wait("qs", "t", st["wst0_free"])
    hl = {0: load_head(0, 0)}
    pslot_free = {}
    for h in range(NH):
        b = h % 2
        if h + 1 < NH:
            p.wait("qs", "t", st.get(("kvf", (h + 1) % 2), 0))
            hl[h + 1] = load_head(h + 1, (h + 1) % 2)
        p.wait("t", hl[h][0], hl[h][1])
        _trips = []
        for _s in range(2):
            for _kb in range(8):
                _qs = 0 if _s == 0 else _kb * 128
                _q = _qs
                while _q < TOK:
                    _qe = min(TOK, (_q // 512 + 1) * 512)
                    _trips.append((_s, _kb, _q, _qe))
                    _q = _qe
        last_lo = max(i for i, t in enumerate(_trips) if t[2] < 512)
        last_hi = max(i for i, t in enumerate(_trips) if t[2] >= 512)
        last_av = len(_trips) - 1
        _ti = 0
        sc_i = 0
        for s in range(2):
            for kb in range(8):
                slot = s * 8 + kb
                qstart = 0 if s == 0 else kb * 128
                even = sc_i % 2 == 0
                ktile = kat(b)[:, slot * 128:(slot + 1) * 128]
                pieces = []
                q = qstart
                while q < TOK:
                    qe = min(TOK, (q // 512 + 1) * 512)
                    pieces.append((q, qe))
                    q = qe

                def psc(qa, qe, ev=even):
                    if ev:
                        return pw0[:, qa:qe]
                    return (p0 if qa < 512 else p1)[:, qa - (qa // 512) * 512:
                                                    qe - (qa // 512) * 512]

                p.wait("t", "s", st.get(("scf", sc_i % 2), 0))
                for i, (qa, qe) in enumerate(pieces):
                    p.mm(psc(qa, qe), ktile, qat(b)[:, qa:qe], True, True,
                         sem="t" if i == len(pieces) - 1 else None,
                         ldw=(i == 0))
                t_sc = p.val("t")
                p.wait("s", "t", t_sc)
                p.wait("s", "t", pslot_free.get(sc_i % 3, 0))
                P = pat(sc_i)
                for i, (qa, qe) in enumerate(pieces):
                    p.s("activation", P[:, qa:qe], psc(qa, qe), AF.Exp,
                        bias=cols[:, 0:1], scale=1.0,
                        sem="s" if i == len(pieces) - 1 else None)
                st[("scf", sc_i % 2)] = p.val("s")
                s_exp = p.val("s")
                p.wait("v", "s", s_exp)
                if s == 0:
                    for qb2 in range(kb):
                        p.v("tensor_mul", P[:, qb2 * 128:(qb2 + 1) * 128],
                            P[:, qb2 * 128:(qb2 + 1) * 128], masks[:, 128:256])
                    p.v("tensor_mul", P[:, kb * 128:(kb + 1) * 128],
                        P[:, kb * 128:(kb + 1) * 128], masks[:, 0:128], sem="v")
                else:
                    p.v("tensor_mul", P[:, kb * 128:(kb + 1) * 128],
                        P[:, kb * 128:(kb + 1) * 128], masks[:, 256:384])
                    for qb2 in range(kb + 1, 8):
                        p.v("tensor_mul", P[:, qb2 * 128:(qb2 + 1) * 128],
                            P[:, qb2 * 128:(qb2 + 1) * 128], masks[:, 384:512])
                    p.v("engine_nop", sem="v")
                v_mask = p.val("v")
                p.wait("t", "v", v_mask)
                first_iter = (sc_i == 0)
                for pi, (qa, qe) in enumerate(pieces):
                    lrow = (p2, p3)[qa // 512]
                    is_last = (_ti + pi) == (last_lo if qa < 512 else last_hi)
                    p.mm(lrow[0:1, qa - (qa // 512) * 512: qe - (qa // 512) * 512],
                         B["ones16"][:, :], P[:, qa:qe], first_iter, is_last)
                vt = vat(b)[:, slot * 128:(slot + 1) * 128]
                for i, (qa, qe) in enumerate(pieces):
                    p.mm(pw1[:, qa:qe], vt, P[:, qa:qe], first_iter,
                         (_ti + i) == (last_lo if qa < 512 else last_hi),
                         sem="t" if i == len(pieces) - 1 else None)
                _ti += len(pieces)
                pslot_free[sc_i % 3] = p.val("t")
                sc_i += 1
        t_head = p.val("t")
        st[("kvf", b)] = t_head
        p.wait("v", "t", t_head)
        p.v("reciprocal", rows[0:1, 0:512], p2[0:1, :])
        p.v("reciprocal", rows[0:1, 512:1024], p3[0:1, :], sem="v")
        v_il = p.val("v")
        p.wait("t", "v", v_il)
        p.wait("t", "s", p.val("s"))
        p.mmf(pw0[:, 0:512], B["onesf"][0:1, :], rows[0:1, 0:512], True, True)
        p.mmf(pw0[:, 512:1024], B["onesf"][0:1, :], rows[0:1, 512:1024], True, True)
        _pad(p, B)
        t_il = p.val("t")
        p.wait("s", "t", t_il)
        p.s("copy", B["repb"][:, 0:512], pw0[:, 0:512])
        p.s("copy", B["repb"][:, 512:1024], pw0[:, 512:1024], sem="s")
        s_il = p.val("s")
        p.wait("v", "s", s_il)
        p.v("tensor_mul", anf(h), pw1[:, :], B["repb"][:, 0:1024], sem="v")
        v_o = p.val("v")
        st[("scf", 0)] = max(st.get(("scf", 0), 0), p.val("s"))
        p.wait("t", "v", v_o)

    st[("pof", 0)] = st[("pof", 1)] = p.val("v")
    p.wait("t", "s", p.val("s"))
    for og in range(4):
        evs = _wload_attnT(p, B, A, st, A["cwo"], og,
                           [B["p0"], B["p1"], B["p2"], B["p3"]], p.val("t"))
        for oc in range(4):
            o = og * 4 + oc
            po = (pw0, pw1)[oc % 2]
            p.wait("t", "v", st.get(("pof", oc % 2), 0))
            p.wait("t", "v", evs[oc])
            for hh in range(16):
                lhs = wst[:, oc * 2048 + hh * 128: oc * 2048 + hh * 128 + 128]
                for qh in range(2):
                    p.mm(po[:, qh * 512:(qh + 1) * 512], lhs,
                         anf(hh)[:, qh * 512:(qh + 1) * 512], hh == 0, hh == 15,
                         sem="t" if (hh == 15 and qh == 1) else None,
                         ldw=(qh == 0))
            _pad(p, B)
            t_o = p.val("t")
            p.wait("v", "t", t_o)
            p.v("scalar_tensor_tensor", xu[:, o * TOK:(o + 1) * TOK], po[:, :],
                bsc[:, 48 + o:48 + o + 1], xu[:, o * TOK:(o + 1) * TOK],
                ALU.add, ALU.add, sem="v")
            st[("pof", oc % 2)] = p.val("v")
    p.barrier()


def epilogue(p, B, A, st):
    vy = p.dma("qg", A["y"].rearrange("(j q) t -> q j t", q=128),
               B["xu"][:, :].rearrange("q (j t) -> q j t", j=16), "qgS")
    p.wait("qg", "qgS", vy)
    p.barrier()



def _host_consts():
    h128 = _hadamard(128).astype(ml_dtypes.bfloat16)
    iden = np.eye(128, dtype=np.float32)
    sel4 = np.zeros((4, TH), np.float32)
    for b in range(4):
        sel4[b, b * 128:(b + 1) * 128] = 1.0
    ones = np.ones((128, 128), np.float32)
    return h128, iden, sel4, ones


def _masks_for(parity):
    tri = np.zeros((128, 128), np.float32)
    for i in range(128):
        tri[i, i:] = 1.0
    onesq = np.ones((128, 128), np.float32)
    zer = np.zeros((128, 128), np.float32)
    if parity == 0:
        tiles = [tri, zer, zer, zer]
    else:
        tiles = [onesq, onesq, tri, onesq]
    return np.concatenate(tiles, axis=1).astype(ml_dtypes.bfloat16)


def kernel(L0_attn_X, L0_attn_Wq, L0_attn_bq, L0_attn_Wk, L0_attn_bk,
           L0_attn_Wv, L0_attn_bv, L0_attn_Wo, L0_attn_bo,
           L0_ffn_w_gate, L0_ffn_w_up, L0_ffn_w_down):
    X = np.ascontiguousarray(np.asarray(L0_attn_X, np.float32))
    WQ = np.ascontiguousarray(np.asarray(L0_attn_Wq, np.float32))
    WK = np.ascontiguousarray(np.asarray(L0_attn_Wk, np.float32))
    WV = np.ascontiguousarray(np.asarray(L0_attn_Wv, np.float32))
    WO = np.ascontiguousarray(np.asarray(L0_attn_Wo, np.float32))
    WG = np.ascontiguousarray(np.asarray(L0_ffn_w_gate, np.float32))
    WU = np.ascontiguousarray(np.asarray(L0_ffn_w_up, np.float32))
    WD = np.ascontiguousarray(np.asarray(L0_ffn_w_down, np.float32))
    bq = np.asarray(L0_attn_bq, np.float32)
    bk = np.asarray(L0_attn_bk, np.float32)
    bv = np.asarray(L0_attn_bv, np.float32)
    bo = np.asarray(L0_attn_bo, np.float32)

    h128, iden, sel4, ones = _host_consts()
    idenb16 = np.eye(128).astype(ml_dtypes.bfloat16)
    bcols = np.zeros((128, 64), np.float32)
    bcols[:, 0:16] = bq.reshape(16, 128).T
    bcols[:, 16:32] = bk.reshape(16, 128).T
    bcols[:, 32:48] = bv.reshape(16, 128).T
    bcols[:, 48:64] = bo.reshape(16, 128).T
    bvrow = bv.reshape(1, D)

    if "nc" not in _CACHE:
        _CACHE["nc"] = build_program()
    nc = _CACHE["nc"]

    in_maps = []
    for c in range(8):
        seq, half = c // 2, c % 2
        xs = np.ascontiguousarray(X[seq, half * TOK:(half + 1) * TOK, :].T)
        wsh = np.concatenate([
            WG[c * 1024:(c + 1) * 1024, :],
            WU[c * 1024:(c + 1) * 1024, :],
            WD[c * 256:(c + 1) * 256, :].reshape(1024, 2048),
        ], axis=0)
        in_maps.append({
            "x0": xs, "wq": WQ, "wk": WK, "wv": WV, "wo": WO,
            "wg": WG, "wu": WU, "wdn": WD,
            "wshard": np.ascontiguousarray(wsh),
            "bcols": bcols, "bvrow": bvrow,
            "h128c": h128, "idenbc": idenb16, "idenc": iden, "sel4c": sel4,
            "onesc": ones,
            "m01": _masks_for(half),
        })

    res = run_bass_kernel_spmd(nc, in_maps, core_ids=list(range(8)))
    out = np.empty((4, 2048, 2048), np.float32)
    for c in range(8):
        seq, half = c // 2, c % 2
        out[seq, half * TOK:(half + 1) * TOK, :] = res.results[c]["y"].T
    if _DBG.get("keep"):
        _DBG["res"] = res
    return out


# version 06 (final)
# speedup vs baseline: 1.0497x; 1.0497x over previous
import math
import numpy as np
import ml_dtypes
import concourse.bass as bass
import concourse.mybir as mybir
from concourse.bass_utils import run_bass_kernel_spmd

F32 = mybir.dt.float32
BF16 = mybir.dt.bfloat16
AF = mybir.ActivationFunctionType
ALU = mybir.AluOpType
AX = mybir.AxisListType

MAGIC = 12582912.0
EPS = 1e-5
M30 = 30.0
D = 2048
TOK = 1024
TH = 512
NCH = 16
NI = 64
NH = 16
ISQ1 = float(np.float32(1.0) / np.float32(np.sqrt(np.float32(2048.0))))
ISQ2 = float(np.float32(1.0) / np.float32(np.sqrt(np.float32(8192.0))))
S4 = float(np.float32(1.0) / np.float32(np.sqrt(np.sqrt(np.float32(128.0)))))

_DBG = {}
_CACHE = {}


def _hadamard(n):
    H = np.array([[1.0]], np.float32)
    while H.shape[0] < n:
        H = np.block([[H, H], [H, -H]])
    return H.astype(np.float32)


QSEMS = {
    "qs": ["qsA", "wl0", "wl1", "sx0", "sx1", "hl0", "hl1"],
    "qg": ["qgA", "qgR", "qgS", "qgC", "mw0", "mw1", "ti0", "ti1", "to0", "to1",
           "tw0", "tw1", "ks0", "ks1", "vs0", "vs1"],
}


class Plan:
    def __init__(self, active, eng, sems):
        self.active = active
        self.eng = eng
        self.sems = sems

    def flush(self, q):
        for sem in QSEMS[q]:
            self.wait(q, sem, self.val(sem))

    def pub(self, q):
        ins = None
        if self.active == q:
            ins = self.eng[q].nop()
        return self._inc(ins, q, 1)

    def _inc(self, ins, sem, k):
        rec = self.sems[sem]
        rec[1] += k
        if ins is not None:
            ins.then_inc(rec[0], k)
        return rec[1]

    def val(self, sem):
        return self.sems[sem][1]

    def wait(self, who, sem, v):
        if v > 0 and self.active == who:
            self.eng[who].wait_ge(self.sems[sem][0], v)

    def dma(self, q, out, in_, sem, transpose=False):
        ins = None
        if self.active == q:
            ins = self.eng[q].dma_start(out, in_, transpose=transpose)
        return self._inc(ins, sem, 16)

    def mm(self, out, lhsT, rhs, start, stop, sem=None, ldw=True):
        ins = None
        if self.active == "t":
            if ldw:
                self.eng["t"].ldweights(lhsT)
            ins = self.eng["t"].matmul(out, lhsT, rhs, start=start, stop=stop)
        if sem is not None:
            return self._inc(ins, sem, 1)
        return None

    def mmf(self, out, lhsT, rhs, start, stop, sem=None):
        ins = None
        if self.active == "t":
            ins = self.eng["t"].matmul(out, lhsT, rhs, start=start, stop=stop)
        if sem is not None:
            return self._inc(ins, sem, 1)
        return None

    def v(self, meth, *args, sem=None, **kw):
        ins = None
        if self.active == "v":
            ins = getattr(self.eng["v"], meth)(*args, **kw)
        if sem is not None:
            return self._inc(ins, sem, 1)
        return None

    def s(self, meth, *args, sem=None, **kw):
        ins = None
        if self.active == "s":
            ins = getattr(self.eng["s"], meth)(*args, **kw)
        if sem is not None:
            return self._inc(ins, sem, 1)
        return None

    def cc(self, kind, op, groups, ins_, outs_):
        ins = None
        if self.active == "qg":
            ins = self.eng["qg"].collective_compute(
                kind, op, replica_groups=groups, ins=ins_, outs=outs_)
        return self._inc(ins, "cc", 1)

    def barrier(self):
        self.flush("qs")
        self.flush("qg")
        targets = {}
        for e in ("t", "v", "s", "qs", "qg"):
            ins = None
            if self.active == e:
                ins = self.eng[e].nop()
            targets[e] = self._inc(ins, e, 1)
        for e in ("t", "v", "s", "qs", "qg"):
            for w in ("t", "v", "s", "qs", "qg"):
                if w != e:
                    self.wait(e, w, targets[w])



def build_program():
    nc = bass.Bass("TRN2", target_bir_lowering=False,
                   detect_race_conditions=not _DBG.get("noraces", False))
    A = {}

    def din(name, shape, dty, kind="ExternalInput"):
        A[name] = nc.dram_tensor(name, shape, dty, kind=kind).ap()

    din("x0", [D, TOK], F32)
    for n in ("wq", "wk", "wv", "wo"):
        din(n, [D, D], F32)
    din("wg", [8192, D], F32)
    din("wu", [8192, D], F32)
    din("wdn", [D, 8192], F32)
    din("wshard", [3072, D], F32)
    din("bcols", [128, 64], F32)
    din("bvrow", [1, D], F32)
    din("h128c", [128, 128], BF16)
    din("idenbc", [128, 128], BF16)
    din("idenc", [128, 128], F32)
    din("sel4c", [4, TH], F32)
    din("onesc", [128, 128], F32)
    din("m01", [128, 512], BF16)
    din("y", [D, TOK], F32, "ExternalOutput")
    din("cgT", [D, 8192], BF16, "Internal")
    din("cuT", [D, 8192], BF16, "Internal")
    din("cdT", [8192, D], BF16, "Internal")
    din("cwq", [D, D], BF16, "Internal")
    din("cwk", [D, D], BF16, "Internal")
    din("cwv", [D, D], BF16, "Internal")
    din("cwo", [D, D], BF16, "Internal")
    din("ar_in", [1, 16], F32, "Internal")
    din("ar_out", [1, 16], F32, "Internal")
    din("srow", [1, 512], F32, "Internal")
    din("k_dram", [D, TOK], BF16, "Internal")
    din("q_dram", [D, TOK], BF16, "Internal")
    din("v_dram", [TOK, D], BF16, "Internal")
    din("k_all", [2 * D, TOK], BF16, "Internal")
    din("v_all", [2 * TOK, D], BF16, "Internal")
    din("x_spill", [D, TOK], F32, "Internal")
    for name, shape, dty in _DBG.get("outs", []):
        din(name, shape, dty, "ExternalOutput")

    import contextlib
    stk = contextlib.ExitStack()
    with stk:
        sb = lambda *a: stk.enter_context(nc.sbuf_tensor(*a))
        ps = lambda *a: stk.enter_context(nc.psum_tensor(*a))
        sm = lambda n: stk.enter_context(nc.semaphore(n))
        xu = sb("xu", [128, NCH * TOK], F32)
        a2 = sb("a2", [128, NCH * TOK], BF16)
        wst = sb("wst", [128, NCH * TOK], BF16)
        wf32 = sb("wf32", [128, 2 * 1024], F32)
        tb16 = sb("tb16", [128, 2 * 1024], BF16)
        tmp4 = sb("tmp4", [128, 4 * TH], BF16)
        silub = sb("silu", [128, 4 * TH], F32)
        repb = sb("repb", [128, 4 * TH], F32)
        rw = sb("rw", [128, 2048], F32)
        rows = sb("rows", [1, 2048], F32)
        bvr = sb("bvr", [1, 2048], F32)
        quad = sb("quad", [4, 1024], F32)
        sel4 = sb("sel4", [4, TH], F32)
        hest = sb("hest", [128, 128], BF16)
        idenb = sb("idenb", [128, 128], BF16)
        iden = sb("iden", [128, 128], F32)
        onesf = sb("onesf", [128, 128], F32)
        ones16 = sb("ones16", [128, 1], BF16)
        masks = sb("masks", [128, 512], BF16)
        bco = sb("bco", [128, 64], F32)
        bsc = sb("bsc", [128, 64], F32)
        cols = sb("cols", [128, 8], F32)
        p0 = ps("p0", [128, 512], F32)
        p1 = ps("p1", [128, 512], F32)
        p2 = ps("p2", [128, 512], F32)
        p3 = ps("p3", [128, 512], F32)
        pw0 = ps("pw0", [128, 1024], F32)
        pw1 = ps("pw1", [128, 1024], F32)
        semnames = ["t", "v", "s", "qs", "qg", "cc",
                    "qsA", "wl0", "wl1", "sx0", "sx1", "hl0", "hl1",
                    "qgA", "qgR", "qgS", "qgC", "mw0", "mw1", "ti0", "ti1",
                    "to0", "to1", "tw0", "tw1", "ks0", "ks1", "vs0", "vs1"]
        shandles = {n: sm("m_" + n) for n in semnames}
        block = stk.enter_context(nc.Block())
        B = dict(xu=xu, a2=a2, wst=wst, wf32=wf32, tb16=tb16, tmp4=tmp4,
                 silu=silub, repb=repb, rw=rw, rows=rows, bvr=bvr, quad=quad,
                 sel4=sel4, hest=hest, idenb=idenb, iden=iden, onesf=onesf,
                 ones16=ones16,
                 masks=masks, bco=bco, bsc=bsc, cols=cols,
                 p0=p0, p1=p1, p2=p2, p3=p3, pw0=pw0, pw1=pw1)
        def run(active):
            eng = {"t": nc.tensor, "v": nc.vector, "s": nc.scalar,
                   "qs": nc.sync, "qg": nc.gpsimd}
            sems = {k: [v, 0] for k, v in shandles.items()}
            emit_all(Plan(active, eng, sems), B, A)

        @block.tensor
        def _(t):
            run("t")

        @block.vector
        def _(v):
            run("v")

        @block.scalar
        def _(s):
            run("s")

        @block.sync
        def _(s):
            run("qs")

        @block.gpsimd
        def _(g):
            run("qg")

    return nc



def _dump(p, B, A, name, ap):
    if name not in _DBG.get("dump_names", ()):
        return
    p.barrier()
    dv = p.dma("qg", A[name], ap, "qgS")
    p.wait("qg", "qgS", dv)
    p.barrier()


def _dumps(p, B, A, items):
    items = [(n, ap) for (n, ap) in items if n in _DBG.get("dump_names", ())]
    if not items:
        return
    p.barrier()
    dv = 0
    for n, ap in items:
        dv = p.dma("qg", A[n], ap, "qgS")
    p.wait("qg", "qgS", dv)
    p.barrier()


Q_RAW = 0
Q_SCL = 128
Q_T1 = 256
Q_T2 = 384
RG = {0: 512, 1: 768}
CU = {0: 640, 1: 896}


def emit_all(p, B, A):
    st = {"wst0_free": 0, "wst1_free": 0}
    ph = _DBG.get("phases", 4)
    prologue(p, B, A, st)
    if ph >= 1:
        ffn_call(p, B, A, st, first=True)
    if ph >= 2:
        attn_call(p, B, A, st)
    if ph >= 3:
        ffn_call(p, B, A, st, first=False)
    epilogue(p, B, A, st)


def prologue(p, B, A, st):
    rw, rows, cols, onesf = B["rw"], B["rows"], B["cols"], B["onesf"]
    p.dma("qg", B["xu"][:, :].rearrange("q (j t) -> q j t", j=16),
          A["x0"].rearrange("(j q) t -> q j t", q=128), "qgA")
    for name, buf in (("h128c", B["hest"]), ("idenc", B["iden"]),
                      ("idenbc", B["idenb"]),
                      ("sel4c", B["sel4"]), ("onesc", B["onesf"]),
                      ("m01", B["masks"]), ("bcols", B["bco"])):
        p.dma("qs", buf[:, :], A[name], "qsA")
    p.dma("qs", B["bvr"][0:1, :], A["bvrow"], "qsA")
    v_const = p.val("qsA")

    p.v("memset", B["ones16"][:, :], 1.0)
    p.v("memset", cols[:, :], 0.0)
    p.v("memset", B["quad"][0:4, :], 0.0)
    p.v("memset", cols[:, 0:1], -M30)
    p.v("memset", cols[:, 1:2], EPS)
    p.wait("v", "qsA", v_const)
    p.v("tensor_scalar", B["bsc"][:, 0:32], B["bco"][:, 0:32], S4, None, ALU.mult)
    p.v("tensor_copy", B["bsc"][:, 48:64], B["bco"][:, 48:64], sem="v")

    wf32 = B["wf32"]
    v_tile = {0: 0, 1: 0}
    for u in range(48):
        m = u // 16
        row, hh = (u % 16) // 2, u % 2
        buf = u % 2
        r0 = (m * 8 + row) * 128
        p.wait("qg", "v", v_tile[buf])
        vq = p.dma("qg", wf32[:, buf * 1024:(buf + 1) * 1024],
                   A["wshard"][r0:r0 + 128, hh * 1024:(hh + 1) * 1024],
                   "mw0" if buf == 0 else "mw1")
        p.wait("v", "mw0" if buf == 0 else "mw1", vq)
        p.v("reduce_sum", rw[:, 16 + u:17 + u],
            wf32[:, buf * 1024:(buf + 1) * 1024],
            axis=AX.X, apply_absolute_value=True, sem="v")
        v_tile[buf] = p.val("v")
    v_mean = p.val("v")
    p.wait("t", "v", v_mean)
    p.wait("t", "qsA", v_const)
    p.mmf(B["p2"][0:1, 0:48], onesf[:, 0:1], rw[:, 16:64], True, True, sem="t")
    t_mean = p.val("t")
    p.wait("v", "t", t_mean)
    p.v("memset", rows[0:1, 0:16], 0.0)
    for m in range(3):
        p.v("reduce_sum", rows[0:1, m:m + 1], B["p2"][0:1, m * 16:(m + 1) * 16],
            axis=AX.X, sem="v" if m == 2 else None)
    v_mrow = p.val("v")
    p.wait("qg", "v", v_mrow)
    vz = p.dma("qg", A["ar_in"], rows[0:1, 0:16], "qgR")
    p.wait("qg", "qgR", vz)
    p.cc("AllReduce", ALU.add, [[0, 1, 2, 3, 4, 5, 6, 7]],
         [A["ar_in"].opt()], [A["ar_out"].opt()])
    p.wait("qg", "cc", p.val("cc"))
    p.v("memset", rows[0:1, 16:528], 1.0, sem="v")
    v_ms = p.val("v")
    p.wait("qg", "v", v_ms)
    vsum = p.dma("qg", rows[0:1, 16:32], A["ar_out"], "qgR")
    p.wait("v", "qgR", vsum)
    p.v("tensor_scalar", rows[0:1, 16:144], rows[0:1, 16:144],
        1.0 / (8192.0 * 2048.0), EPS, ALU.mult, ALU.max)
    p.v("reciprocal", rows[0:1, 144:272], rows[0:1, 16:144])
    p.v("tensor_scalar", rows[0:1, 272:400], rows[0:1, 16:144], 1.0 / 127.0, None,
        ALU.mult)
    p.v("tensor_scalar", rows[0:1, 400:528], rows[0:1, 16:144], 0.5 / 127.0, None,
        ALU.mult, sem="v")
    v_srow = p.val("v")
    p.wait("qg", "v", v_srow)
    vr = p.dma("qg", A["srow"], rows[0:1, 0:512], "qgR")
    p.wait("qg", "qgR", vr)
    p.dma("qg", cols[:, 2:5], A["srow"][0:1, 144:147].partition_broadcast(128), "qgR")
    p.dma("qg", cols[:, 5:7], A["srow"][0:1, 272:274].partition_broadcast(128), "qgR")
    p.dma("qg", cols[:, 7:8], A["srow"][0:1, 402:403].partition_broadcast(128), "qgR")
    _dump(p, B, A, "d_rows", rows[0:1, :])
    _dump(p, B, A, "d_ar", A["ar_out"])
    st["cast_done"] = 0
    p.barrier()



def _ln(p, B, xsl, dst):
    rows, rw, onesf = B["rows"], B["rw"], B["onesf"]
    p2, p3, pw0 = B["p2"], B["p3"], B["pw0"]
    p.wait("t", "v", p.val("v"))
    p.wait("t", "s", p.val("s"))
    t_sq = {0: 0, 1: 0}
    for j in range(NCH):
        buf = j % 2
        p.wait("s", "t", t_sq[buf])
        p.s("activation", rw[:, 1024 + buf * 512:1024 + buf * 512 + 512], xsl(j),
            AF.Square, sem="s")
        sj = p.val("s")
        p.mmf(p2[0:1, 0:TH], onesf[:, 0:1], xsl(j), j == 0, j == NCH - 1)
        p.wait("t", "s", sj)
        p.mmf(p3[0:1, 0:TH], onesf[:, 0:1],
              rw[:, 1024 + buf * 512:1024 + buf * 512 + 512], j == 0, j == NCH - 1,
              sem="t")
        t_sq[buf] = p.val("t")
    _pad(p, B)
    t_stats = p.val("t")
    p.wait("v", "t", t_stats)
    R_M, R_MSQ, R_VAR, R_R = 0, 512, 1024, 1536
    p.v("tensor_scalar", rows[0:1, R_M:R_M + TH], p2[0:1, 0:TH], 1.0 / D, None,
        ALU.mult)
    p.v("tensor_scalar", rows[0:1, R_MSQ:R_MSQ + TH], p3[0:1, 0:TH], 1.0 / D, None,
        ALU.mult)
    p.v("tensor_mul", rows[0:1, R_VAR:R_VAR + TH], rows[0:1, R_M:R_M + TH],
        rows[0:1, R_M:R_M + TH])
    p.v("tensor_sub", rows[0:1, R_VAR:R_VAR + TH], rows[0:1, R_MSQ:R_MSQ + TH],
        rows[0:1, R_VAR:R_VAR + TH], sem="v")
    v_var = p.val("v")
    p.wait("s", "v", v_var)
    p.s("activation", rows[0:1, R_MSQ:R_MSQ + TH], rows[0:1, R_VAR:R_VAR + TH],
        AF.Sqrt, bias=B["cols"][0:1, 1:2], sem="s")
    s_sd = p.val("s")
    p.wait("v", "s", s_sd)
    p.v("reciprocal", rows[0:1, R_R:R_R + TH], rows[0:1, R_MSQ:R_MSQ + TH], sem="v")
    v_rows = p.val("v")
    p.wait("t", "v", v_rows)
    p.wait("t", "s", p.val("s"))
    p.mmf(pw0[:, 0:512], onesf[0:1, :], rows[0:1, R_R:R_R + TH], True, True)
    p.mmf(pw0[:, 512:1024], onesf[0:1, :], rows[0:1, R_M:R_M + TH], True, True)
    _pad(p, B)
    t_rep = p.val("t")
    p.wait("s", "t", t_rep)
    p.s("copy", B["repb"][:, 0:512], pw0[:, 0:512])
    p.s("copy", B["repb"][:, 512:1024], pw0[:, 512:1024], sem="s")
    s_rep = p.val("s")
    p.wait("v", "s", s_rep)
    for j in range(NCH):
        buf = j % 2
        p.v("tensor_sub", rw[:, buf * 512:buf * 512 + 512], xsl(j),
            B["repb"][:, 512:1024])
        p.v("tensor_mul", dst(j), rw[:, buf * 512:buf * 512 + 512],
            B["repb"][:, 0:512], sem="v")
    return p.val("v")


def _radix4(p, buf_of, nch, width, tmp):
    levels = {16: 2, 64: 3}[nch]
    for lv in range(levels):
        s = 4 ** lv
        for base in range(nch):
            if (base // s) % 4 != 0:
                continue
            c0, c1, c2, c3 = (buf_of(base), buf_of(base + s),
                              buf_of(base + 2 * s), buf_of(base + 3 * s))
            T = [tmp[:, i * width:(i + 1) * width] for i in range(4)]
            p.v("tensor_add", T[0], c0, c1)
            p.v("tensor_sub", T[1], c0, c1)
            p.v("tensor_add", T[2], c2, c3)
            p.v("tensor_sub", T[3], c2, c3)
            p.v("tensor_add", c0, T[0], T[2])
            p.v("tensor_add", c1, T[1], T[3])
            p.v("tensor_sub", c2, T[0], T[2])
            p.v("tensor_sub", c3, T[1], T[3], sem="v")
    return p.val("v")


def _quant(p, B, src_of, nchunks, scale_pre, dst_of, cb):
    rw, quad, iden, sel4 = B["rw"], B["quad"], B["iden"], B["sel4"]
    hest = B["hest"]
    p0, p1, pw1 = B["p0"], B["p1"], B["pw1"]
    pf_free = {0: p.val("v"), 1: p.val("v")}
    for c in range(nchunks):
        buf = c % 2
        pf = (p0, p1)[buf]
        p.wait("t", "v", pf_free[buf])
        tv = p.mm(pf[:, :], hest[:, :], src_of(c), True, True, sem="t",
                  ldw=(c == 0))
        p.wait("v", "t", tv)
        if c == 0:
            p.v("tensor_copy", rw[:, 0:512], pf[:, :])
            p.v("tensor_copy", rw[:, 512:1024], pf[:, :], sem="v")
        else:
            p.v("tensor_tensor", rw[:, 0:512], rw[:, 0:512], pf[:, :], ALU.max)
            p.v("tensor_tensor", rw[:, 512:1024], rw[:, 512:1024], pf[:, :],
                ALU.min, sem="v")
        pf_free[buf] = p.val("v")
    _vgap(p, B, 1)
    p.v("tensor_scalar", rw[:, 512:1024], rw[:, 512:1024], -1.0, None, ALU.mult)
    p.v("tensor_tensor", rw[:, 1024:1536], rw[:, 0:512], rw[:, 512:1024], ALU.max,
        sem="v")
    v_mm = p.val("v")
    p.wait("t", "v", v_mm)
    for b in range(4):
        p.mmf(B["p2"][:, b * 128:(b + 1) * 128],
              rw[:, 1024 + b * 128:1024 + (b + 1) * 128], iden[:, :], True, True)
    _pad(p, B)
    t_tp = p.val("t")
    p.wait("v", "t", t_tp)
    for b in range(4):
        p.v("reduce_max", rw[:, 1536 + b:1537 + b], B["p2"][:, b * 128:(b + 1) * 128],
            axis=AX.X, sem="v" if b == 3 else None)
    v_ct = p.val("v")
    p.wait("t", "v", v_ct)
    p.mmf(B["p3"][0:4, 0:128], rw[:, 1536:1540], iden[:, :], True, True)
    _pad(p, B)
    t_qr = p.val("t")
    p.wait("v", "t", t_qr)
    p.v("tensor_scalar", quad[0:4, Q_RAW:Q_RAW + 128], B["p3"][0:4, 0:128],
        scale_pre, None, ALU.mult, sem="v")
    cb()
    v_sc = p.v("engine_nop", sem="v")
    p.wait("t", "v", v_sc)
    p.wait("t", "s", p.val("s"))
    for b in range(4):
        p.mmf(pw1[:, b * 128:(b + 1) * 128], sel4[:, b * 128:(b + 1) * 128],
              quad[0:4, Q_SCL:Q_SCL + 128], True, True)
    _pad(p, B)
    t_rp = p.val("t")
    p.wait("s", "t", t_rp)
    p.s("copy", B["repb"][:, 2 * 512:3 * 512], pw1[:, 0:512], sem="s")
    s_rp = p.val("s")
    p.wait("v", "s", s_rp)
    vv = {0: p.val("v"), 1: p.val("v")}
    for c in range(nchunks):
        buf = c % 2
        pf = (p0, p1)[buf]
        p.wait("t", "v", vv[buf])
        tv = p.mm(pf[:, :], hest[:, :], src_of(c), True, True, sem="t",
                  ldw=(c == 0))
        p.wait("v", "t", tv)
        p.v("tensor_mul", rw[:, buf * 512:buf * 512 + 512], pf[:, :],
            B["repb"][:, 2 * 512:3 * 512], sem="v")
        vv[buf] = p.val("v")
        if c > 0:
            pb = (c - 1) % 2
            p.v("tensor_scalar", dst_of(c - 1), rw[:, pb * 512:pb * 512 + 512],
                MAGIC, MAGIC, ALU.add, ALU.subtract, sem="v")
    _vgap(p, B, 1)
    lb = (nchunks - 1) % 2
    p.v("tensor_scalar", dst_of(nchunks - 1), rw[:, lb * 512:lb * 512 + 512],
        MAGIC, MAGIC, ALU.add, ALU.subtract, sem="v")
    return p.val("v")


def _pad(p, B):
    return p.mmf(B["p0"][:, 0:128], B["onesf"][0:1, :], B["rows"][0:1, 0:128],
                 True, True, sem="t")


def _vgap(p, B, n=1):
    for _ in range(n):
        p.v("tensor_scalar", B["rows"][0:1, 512:1536], B["rows"][0:1, 512:1536],
            1.0, None, ALU.mult)


def _repl_quad(p, B, qoff, dst_off):
    quad, sel4, pw1 = B["quad"], B["sel4"], B["pw1"]
    p.wait("t", "v", p.val("v"))
    p.wait("t", "s", p.val("s"))
    for b in range(4):
        p.mmf(pw1[:, b * 128:(b + 1) * 128], sel4[:, b * 128:(b + 1) * 128],
              quad[0:4, qoff:qoff + 128], True, True)
    _pad(p, B)
    t_r = p.val("t")
    p.wait("s", "t", t_r)
    p.s("copy", B["repb"][:, dst_off:dst_off + 512], pw1[:, 0:512], sem="s")
    s_r = p.val("s")
    p.wait("v", "s", s_r)
    return s_r


def _wload_blockT(p, B, A, cacheT, r0, c0, buf):
    wst = B["wst"]
    sem = "wl0" if buf == 0 else "wl1"
    dst = wst[:, buf * 8192:(buf + 1) * 8192].rearrange("q (j t) -> q j t", j=16)
    srcv = cacheT[r0:r0 + 2048, c0:c0 + 512].rearrange("(j q) t -> q j t", q=128)
    v = p.dma("qs", dst, srcv, sem)
    return sem, v


def _wload_block(p, B, A, cache, r0, r1, buf):
    wst = B["wst"]
    sem = "wl0" if buf == 0 else "wl1"
    v = 0
    for h in range(16):
        dst = wst[:, buf * 8192 + h * 512: buf * 8192 + h * 512 + (r1 - r0)]
        v = p.dma("qs", dst, cache[r0:r1, h * 128:(h + 1) * 128], sem,
                  transpose=True)
    return sem, v


def _wload_attnT(p, B, A, st, cache, og, pwslices, wst_free):
    wst, idenb = B["wst"], B["idenb"]
    wfb = B["wf32"][:, :].bitcast(BF16)
    p.wait("v", "t", wst_free)
    evs = []
    for t4 in range(4):
        b2 = t4 % 2
        lsem = "wl0" if b2 == 0 else "wl1"
        p.wait("qs", "v", st.get(("awl_evac", b2), 0))
        lv = p.dma("qs", wfb[:, b2 * 2048:(b2 + 1) * 2048],
                   cache[og * 512 + t4 * 128: og * 512 + (t4 + 1) * 128, :],
                   lsem)
        p.wait("t", lsem, lv)
        p.wait("t", "v", st.get("awl_pfree", 0))
        for j in range(16):
            ps = pwslices[j // 4]
            p.mm(ps[:, (j % 4) * 128:(j % 4 + 1) * 128],
                 wfb[:, b2 * 2048 + j * 128: b2 * 2048 + (j + 1) * 128],
                 idenb[:, :], True, True, sem="t" if j == 15 else None)
        tv = p.val("t")
        p.wait("v", "t", tv)
        for k in range(4):
            p.v("tensor_copy",
                wst[:, t4 * 2048 + k * 512: t4 * 2048 + (k + 1) * 512],
                pwslices[k][:, 0:512], sem="v" if k == 3 else None)
        st[("awl_evac", b2)] = p.val("v")
        st["awl_pfree"] = p.val("v")
        evs.append(p.val("v"))
    return evs


def _tern_pair(p, B, A, st, wsrc, cdstT, otile, hhalf, icol):
    wf32, cols, rw = B["wf32"], B["cols"], B["rw"]
    idenb = B["idenb"]
    pw0, pw1 = B["pw0"], B["pw1"]
    rwb = rw[:, :].bitcast(BF16)
    buf = st["tern_k"] & 1
    st["tern_k"] += 1
    ti = ("ti0", "ti1")[buf]
    p.wait("qg", "v", st["tern_vdone"])
    vq = p.dma("qg", wf32[:, :].rearrange("q (b t) -> q b t", b=2),
               wsrc[otile * 128:(otile + 2) * 128,
                    hhalf * 1024:(hhalf + 1) * 1024]
               .rearrange("(b q) t -> q b t", q=128), ti)
    p.wait("v", ti, vq)
    p.wait("v", "t", st["tern_tdone"].get(buf, 0))
    sl = wf32[:, :]
    ol = (B["tb16"], B["tmp4"])[buf][:, :]
    p.v("tensor_scalar", sl, sl, cols[:, icol:icol + 1], 1.49, ALU.mult, ALU.min)
    p.v("tensor_scalar", sl, sl, -1.49, MAGIC, ALU.max, ALU.add)
    p.v("tensor_scalar", ol, sl, MAGIC, None, ALU.subtract, sem="v")
    st["tern_vdone"] = p.val("v")
    p.wait("t", "v", st["tern_vdone"])
    p.wait("t", "v", st["tern_pevac"])
    for b in range(2):
        for j in range(8):
            pw = pw0 if j < 4 else pw1
            c0 = (j % 4) * 256 + b * 128
            p.mm(pw[:, c0:c0 + 128], ol[:, b * 1024 + j * 128:
                                         b * 1024 + (j + 1) * 128],
                 idenb[:, :], True, True,
                 sem="t" if (b == 1 and j == 7) else None)
    st["tern_tdone"][buf] = p.val("t")
    p.wait("v", "t", st["tern_tdone"][buf])
    p.wait("v", "to0", st["tern_sdone"])
    p.v("tensor_copy", rwb[:, 2048:3072], pw0[:, 0:1024])
    p.v("tensor_copy", rwb[:, 3072:4096], pw1[:, 0:1024], sem="v")
    st["tern_pevac"] = p.val("v")
    p.wait("qg", "v", st["tern_pevac"])
    base = otile * 128
    dst = cdstT[hhalf * 1024:(hhalf + 1) * 1024, base:base + 256]
    od = p.dma("qg", dst.rearrange("(j q) c -> q j c", q=128),
               rwb[:, 2048:4096].rearrange("q (j c) -> q j c", j=8), "to0")
    st["tern_sdone"] = od
    return od


def _tern_og(p, B, A, st, which, og):
    wsrc = A["wg"] if which == "g" else A["wu"]
    cdstT = A["cgT"] if which == "g" else A["cuT"]
    icol = 2 if which == "g" else 3
    for hh in range(2):
        for otp in range(2):
            _tern_pair(p, B, A, st, wsrc, cdstT, og * 4 + otp * 2, hh, icol)
    p.wait("qg", "to0", st["tern_sdone"])
    st["cg_prog" if which == "g" else "cu_prog"].append(p.pub("qg"))


def _tern_sched(p, B, A, st, first, half, og):
    if not (first and half == 0):
        return []
    q = []
    if len(st["cg_prog"]) <= og + 1 and og + 1 < 16:
        for which in ("g", "u"):
            wsrc = A["wg"] if which == "g" else A["wu"]
            cdstT = A["cgT"] if which == "g" else A["cuT"]
            icol = 2 if which == "g" else 3
            for hh in range(2):
                for otp in range(2):
                    q.append(("pair", wsrc, cdstT, (og + 1) * 4 + otp * 2, hh,
                              icol))
            q.append(("pub", which))
    for k2 in range(4):
        idx = og * 4 + k2
        hh2, otp = idx // 8, idx % 8
        q.append(("pair", A["wdn"], A["cdT"], 2 * otp, hh2, 4))
    return q


def _tern_drain(p, B, A, st, q, n):
    for _ in range(min(n, len(q))):
        it = q.pop(0)
        if it[0] == "pair":
            _, wsrc, cdstT, ot, hh, icol = it
            _tern_pair(p, B, A, st, wsrc, cdstT, ot, hh, icol)
        else:
            p.wait("qg", "to0", st["tern_sdone"])
            st["cg_prog" if it[1] == "g" else "cu_prog"].append(p.pub("qg"))


def ffn_call(p, B, A, st, first):
    xu, a2, wst = B["xu"], B["a2"], B["wst"]
    rw, quad, cols, silub = B["rw"], B["quad"], B["cols"], B["silu"]
    p0, p1, p2, p3 = B["p0"], B["p1"], B["p2"], B["p3"]
    ub = xu[:, :].bitcast(BF16)

    if first:
        st["tern_k"] = 0
        st["tern_vdone"] = 0
        st["tern_tdone"] = {}
        st["tern_pevac"] = 0
        st["tern_sdone"] = 0
        st["cg_prog"] = []
        st["cu_prog"] = []

    def xsl(j, half):
        return xu[:, j * TOK + half * TH: j * TOK + half * TH + TH]

    def q1(j, half):
        return a2[:, half * 8192 + j * TH: half * 8192 + (j + 1) * TH]

    for half in range(2):
        _ln(p, B, lambda j, h=half: xsl(j, h), lambda j, h=half: q1(j, h))
        _radix4(p, lambda j, h=half: q1(j, h), 16, TH, B["tmp4"])

        def q1_cb(h=half):
            p.v("tensor_scalar", quad[0:4, Q_T1:Q_T1 + 128],
                quad[0:4, Q_RAW:Q_RAW + 128], 1.0, EPS, ALU.mult, ALU.max)
            p.v("tensor_scalar", quad[0:4, Q_T2:Q_T2 + 128],
                quad[0:4, Q_T1:Q_T1 + 128], 1.0 / (127.0 * ISQ1), None, ALU.mult)
            p.v("tensor_scalar", quad[0:4, RG[h]:RG[h] + 128],
                quad[0:4, Q_T1:Q_T1 + 128], cols[0:4, 5:6], None, ALU.mult)
            p.v("tensor_scalar", quad[0:4, CU[h]:CU[h] + 128],
                quad[0:4, Q_T1:Q_T1 + 128], cols[0:4, 6:7], None, ALU.mult)
            p.v("reciprocal", quad[0:4, Q_SCL:Q_SCL + 128],
                quad[0:4, Q_T2:Q_T2 + 128], sem="v")
            _vgap(p, B, 3)

        if _DBG.get("split_q1") and half == 0:
            _quant(p, B, lambda j, h=half: q1(j, h), 16, ISQ1,
                   lambda j: a2[:, 8192 + j * TH: 8192 + (j + 1) * TH], q1_cb)
        else:
            _quant(p, B, lambda j, h=half: q1(j, h), 16, ISQ1,
                   lambda j, h=half: q1(j, h), q1_cb)
        if half == 0:
            _dumps(p, B, A, [
                ("d_an", a2[:, 0:8192]),
                ("d_q1h0", a2[:, 8192:16384] if _DBG.get("split_q1")
                 else a2[:, 0:8192]),
                ("d_repb0", B["repb"][:, :]),
                ("d_rw0", B["rw"][:, :]),
                ("d_quad0", B["quad"][0:4, :]),
            ])

    _dumps(p, B, A, [
        ("d_q1", a2[:, :]),
        ("d_repb", B["repb"][:, :]),
        ("d_quad", B["quad"][0:4, :]),
        ("d_cols", cols[:, :]),
    ])
    v_fin = p.val("v")
    p.wait("qg", "v", v_fin)
    spill_v = p.dma("qg", A["x_spill"].rearrange("(j q) t -> q j t", q=128),
                    xu[:, :].rearrange("q (j t) -> q j t", j=16), "qgS")
    st["spill_v"] = spill_v

    if first and not st["cg_prog"]:
        _tern_og(p, B, A, st, "g", 0)
        _tern_og(p, B, A, st, "u", 0)

    for half in range(2):
        hoff = half * TH
        p.wait("t", "to0", st.get("tern_sdone", 0))
        _repl_quad(p, B, RG[half], 3 * 512)

        def uch(c):
            return ub[:, c * TH:(c + 1) * TH]

        for i2 in range(2):
            st[("pgf", i2)] = st[("puf", i2)] = p.val("v")
        first_u_guard = True
        for og in range(16):
            if first:
                while len(st["cg_prog"]) <= og:
                    _tern_og(p, B, A, st, "g", len(st["cg_prog"]))
                    _tern_og(p, B, A, st, "u", len(st["cu_prog"]))
                p.wait("qs", "qg", st["cg_prog"][og])
            tq = _tern_sched(p, B, A, st, first, half, og)
            p.wait("qs", "t", st["wst0_free"])
            g_sem, g_val = _wload_blockT(p, B, A, A["cgT"], 0, og * 512, 0)
            for oc in range(4):
                pg = (p0, p1)[oc % 2]
                p.wait("t", "v", st.get(("pgf", oc % 2), 0))
                p.wait("t", g_sem, g_val)
                for h in range(16):
                    lhs = wst[:, h * 512 + oc * 128: h * 512 + oc * 128 + 128]
                    p.mm(pg[:, :], lhs, q1(h, half), h == 0, h == 15,
                         sem="t" if h == 15 else None)
                t_g = p.val("t")
                p.wait("v", "t", t_g)
                p.wait("v", "s", st.get(("rwg", oc % 2), 0))
                p.v("tensor_mul", rw[:, (oc % 2) * 512:(oc % 2) * 512 + 512],
                    pg[:, :], B["repb"][:, 3 * 512:4 * 512], sem="v")
                st[("pgf", oc % 2)] = p.val("v")
                p.wait("s", "v", p.val("v"))
                p.wait("s", "v", st.get(("silf", oc), 0))
                p.s("activation", silub[:, oc * 512:(oc + 1) * 512],
                    rw[:, (oc % 2) * 512:(oc % 2) * 512 + 512], AF.Silu, sem="s")
                st[("rwg", oc % 2)] = p.val("s")
                st[("sil_rdy", oc)] = p.val("s")
                _tern_drain(p, B, A, st, tq, 2)
            st["wst0_free"] = p.val("t")
            if first:
                p.wait("qs", "qg", st["cu_prog"][og])
            p.wait("qs", "t", st["wst1_free"])
            u_sem, u_val = _wload_blockT(p, B, A, A["cuT"], 0, og * 512, 1)
            for oc in range(4):
                pu = (p2, p3)[oc % 2]
                p.wait("t", "v", st.get(("puf", oc % 2), 0))
                p.wait("t", u_sem, u_val)
                for h in range(16):
                    lhs = wst[:, 8192 + h * 512 + oc * 128:
                              8192 + h * 512 + oc * 128 + 128]
                    p.mm(pu[:, :], lhs, q1(h, half), h == 0, h == 15,
                         sem="t" if h == 15 else None)
                t_u = p.val("t")
                p.wait("v", "t", t_u)
                p.wait("v", "s", st.get(("sil_rdy", oc), 0))
                if first_u_guard:
                    p.wait("v", "qgS", st["spill_v"])
                    first_u_guard = False
                p.v("tensor_mul", uch(og * 4 + oc),
                    silub[:, oc * 512:(oc + 1) * 512], pu[:, :], sem="v")
                st[("puf", oc % 2)] = p.val("v")
                st[("silf", oc)] = p.val("v")
                _tern_drain(p, B, A, st, tq, 2)
            st["wst1_free"] = p.val("t")
            _tern_drain(p, B, A, st, tq, 99)
            if first and half == 0 and og == 15:
                p.wait("qg", "to0", st["tern_sdone"])
                st["cd_done"] = p.pub("qg")

        if half == 0:
            _dump(p, B, A, "d_u", xu[:, :].bitcast(BF16))

        p.wait("v", "t", max(st.get("tern_tdone", {}).values(), default=0))
        _radix4(p, uch, 64, TH, B["tmp4"])

        if half == 0:
            _dump(p, B, A, "d_uh", xu[:, :].bitcast(BF16))

        def q2_cb(h=half):
            p.v("tensor_mul", quad[0:4, Q_T1:Q_T1 + 128],
                quad[0:4, Q_RAW:Q_RAW + 128], quad[0:4, CU[h]:CU[h] + 128])
            _vgap(p, B, 1)
            p.v("tensor_scalar", quad[0:4, Q_T1:Q_T1 + 128],
                quad[0:4, Q_T1:Q_T1 + 128], 1.0, EPS, ALU.mult, ALU.max)
            p.v("tensor_scalar", quad[0:4, Q_T2:Q_T2 + 128],
                quad[0:4, Q_T1:Q_T1 + 128], cols[0:4, 7:8], None, ALU.mult)
            p.v("tensor_scalar", quad[0:4, Q_RAW:Q_RAW + 128],
                quad[0:4, Q_T1:Q_T1 + 128], 1.0 / (127.0 * ISQ2), None, ALU.mult)
            p.v("reciprocal", quad[0:4, Q_T1:Q_T1 + 128],
                quad[0:4, Q_RAW:Q_RAW + 128], sem="v")
            _vgap(p, B, 3)
            p.v("tensor_mul", quad[0:4, Q_SCL:Q_SCL + 128],
                quad[0:4, Q_T1:Q_T1 + 128], quad[0:4, CU[h]:CU[h] + 128], sem="v")
            _vgap(p, B, 1)

        p.wait("t", "to0", st.get("tern_sdone", 0))
        _quant(p, B, uch, 64, ISQ2, uch, q2_cb)
        if half == 0:
            _dump(p, B, A, "d_q2", xu[:, :].bitcast(BF16))
            _dump(p, B, A, "d_quad2", B["quad"][0:4, :])
        _repl_quad(p, B, Q_T2, 3 * 512)

        for i2 in range(4):
            st[("pdf", i2)] = p.val("v")
        for odg in range(4):
            for i in range(4):
                p.wait("t", "v", st.get(("pdf", i), 0))
            kq_t = {0: st["wst0_free"], 1: st["wst1_free"]}
            for kq in range(4):
                buf = kq % 2
                p.wait("qs", "t", kq_t[buf])
                if first:
                    p.wait("qs", "qg", st["cd_done"])
                dsem, d_val = _wload_blockT(p, B, A, A["cdT"], kq * 2048,
                                            odg * 512, buf)
                p.wait("t", dsem, d_val)
                for oc in range(4):
                    pd = (p0, p1, p2, p3)[oc]
                    for h in range(16):
                        h2 = kq * 16 + h
                        lhs = wst[:, buf * 8192 + h * 512 + oc * 128:
                                  buf * 8192 + h * 512 + oc * 128 + 128]
                        p.mm(pd[:, :], lhs, uch(h2), (kq == 0) and (h == 0),
                             (kq == 3) and (h == 15),
                             sem="t" if h == 15 else None)
                kq_t[buf] = p.val("t")
            st["wst0_free"] = kq_t[0]
            st["wst1_free"] = kq_t[1]
            t_dn = p.val("t")
            for oc in range(4):
                pd = (p0, p1, p2, p3)[oc]
                j = odg * 4 + oc
                sx = rw[:, 1024 + (oc % 2) * 512: 1024 + (oc % 2) * 512 + 512]
                t1 = rw[:, (oc % 2) * 512:(oc % 2) * 512 + 512]
                sxs = "sx0" if oc % 2 == 0 else "sx1"
                tws = "tw0" if oc % 2 == 0 else "tw1"
                p.wait("qs", "v", st.get(("sxf", oc % 2), 0))
                vsx = p.dma("qs", sx, A["x_spill"][j * 128:(j + 1) * 128,
                                                   hoff:hoff + TH], sxs)
                p.wait("v", "t", t_dn)
                p.wait("v", sxs, vsx)
                p.wait("v", tws, st.get(("t1f", oc % 2), 0))
                p.v("tensor_mul", t1, pd[:, :], B["repb"][:, 3 * 512:4 * 512])
                p.v("tensor_add", t1, t1, sx, sem="v")
                st[("pdf", oc)] = p.val("v")
                st[("sxf", oc % 2)] = p.val("v")
                v_t1 = p.val("v")
                p.wait("qg", "v", v_t1)
                st[("t1f", oc % 2)] = p.dma(
                    "qg", A["x_spill"][j * 128:(j + 1) * 128, hoff:hoff + TH], t1,
                    tws)
        if first and half == 0:
            for wsrc, wdst in (("wq", "cwq"), ("wk", "cwk"), ("wv", "cwv"),
                               ("wo", "cwo")):
                st["cast_done"] = p.dma("qg", A[wdst], A[wsrc], "qgC")

    p.flush("qg")
    p.wait("qg", "t", p.val("t"))
    rst = p.dma("qg", xu[:, :].rearrange("q (j t) -> q j t", j=16),
                A["x_spill"].rearrange("(j q) t -> q j t", q=128), "qgS")
    p.wait("qg", "qgS", rst)
    p.barrier()


def attn_call(p, B, A, st):
    xu, a2, wst = B["xu"], B["a2"], B["wst"]
    rw, rows, cols = B["rw"], B["rows"], B["cols"]
    p0, p1, p2, p3, pw0, pw1 = B["p0"], B["p1"], B["p2"], B["p3"], B["pw0"], B["pw1"]
    masks, bsc, tb16 = B["masks"], B["bsc"], B["tb16"]

    def anf(j):
        return a2[:, j * TOK:(j + 1) * TOK]

    for half in range(2):
        _ln(p, B,
            lambda j, h=half: xu[:, j * TOK + h * TH: j * TOK + h * TH + TH],
            lambda j, h=half: a2[:, j * TOK + h * TH: j * TOK + h * TH + TH])

    p.wait("qs", "qgC", st["cast_done"])
    p.wait("t", "v", p.val("v"))
    st[("pqf", 0)] = st[("pqf", 1)] = p.val("s")

    for wc, bcb, ddst in (("cwq", 0, "q_dram"), ("cwk", 16, "k_dram")):
        for og in range(4):
            evs = _wload_attnT(p, B, A, st, A[wc], og,
                               [B["p0"], B["p1"], B["p2"], B["p3"]],
                               st["wst0_free"])
            for oc in range(4):
                o = og * 4 + oc
                pq = (pw0, pw1)[oc % 2]
                p.wait("t", "s", st.get(("pqf", oc % 2), 0))
                p.wait("t", "v", evs[oc])
                for h in range(16):
                    lhs = wst[:, oc * 2048 + h * 128: oc * 2048 + h * 128 + 128]
                    for qh in range(2):
                        p.mm(pq[:, qh * 512:(qh + 1) * 512], lhs,
                             anf(h)[:, qh * 512:(qh + 1) * 512], h == 0, h == 15,
                             sem="t" if (h == 15 and qh == 1) else None,
                             ldw=(qh == 0))
                t_q = p.val("t")
                p.wait("s", "t", t_q)
                ks = tb16[:, (oc % 2) * 1024:(oc % 2) * 1024 + 1024]
                kss = "ks0" if oc % 2 == 0 else "ks1"
                p.wait("s", kss, st.get(("ksf", oc % 2), 0))
                p.s("activation", ks, pq[:, :], AF.Identity,
                    bias=bsc[:, bcb + o:bcb + o + 1], scale=S4, sem="s")
                s_k = p.val("s")
                p.wait("qg", "s", s_k)
                st[("ksf", oc % 2)] = p.dma(
                    "qg", A[ddst][o * 128:(o + 1) * 128, :], ks, kss)
                st[("pqf", oc % 2)] = p.val("s")
            st["wst0_free"] = p.val("t")

    st["vs_init"] = (p.val("ks0"), p.val("ks1"))
    for dq in range(4):
        p.wait("t", "v", p.val("v"))
        p.wait("t", "s", p.val("s"))
        p.mmf(pw0[:, 0:512], B["onesf"][0:1, :], B["bvr"][0:1, dq * 512:(dq + 1) * 512],
              True, True)
        _pad(p, B)
        t_bv = p.val("t")
        p.wait("s", "t", t_bv)
        p.s("copy", B["repb"][:, 2 * 512:3 * 512], pw0[:, 0:512], sem="s")
        s_bv = p.val("s")
        p.wait("v", "s", s_bv)
        p.wait("qs", "t", st["wst0_free"])
        lsem, lval = _wload_block(p, B, A, A["cwv"], dq * 512, dq * 512 + 512, 0)
        for tb in range(8):
            pv = (p0, p1, p2, p3)[tb % 4]
            p.wait("t", "v", st.get(("pvf", tb % 4), 0))
            p.wait("t", lsem, lval)
            for h in range(16):
                lhs = anf(h)[:, tb * 128:(tb + 1) * 128]
                rhsw = wst[:, h * 512:(h + 1) * 512]
                p.mm(pv[:, :], lhs, rhsw, h == 0, h == 15,
                     sem="t" if h == 15 else None)
            t_v = p.val("t")
            p.wait("v", "t", t_v)
            vs = tb16[:, (tb % 2) * 1024:(tb % 2) * 1024 + 512]
            vss = "vs0" if tb % 2 == 0 else "vs1"
            if dq == 0 and tb < 2:
                p.wait("v", "ks0", st["vs_init"][0])
                p.wait("v", "ks1", st["vs_init"][1])
            p.wait("v", vss, st.get(("vsf", tb % 2), 0))
            p.v("tensor_add", vs, pv[:, :], B["repb"][:, 2 * 512:3 * 512], sem="v")
            st[("pvf", tb % 4)] = p.val("v")
            v_vs = p.val("v")
            p.wait("qg", "v", v_vs)
            st[("vsf", tb % 2)] = p.dma(
                "qg", A["v_dram"][tb * 128:(tb + 1) * 128, dq * 512:(dq + 1) * 512],
                vs, vss)
        st["wst0_free"] = p.val("t")

    p.flush("qg")
    p.cc("AllGather", ALU.bypass, [[0, 1], [2, 3], [4, 5], [6, 7]],
         [A["k_dram"].opt()], [A["k_all"].opt()])
    p.cc("AllGather", ALU.bypass, [[0, 1], [2, 3], [4, 5], [6, 7]],
         [A["v_dram"].opt()], [A["v_all"].opt()])
    cc_done = p.val("cc")
    p.wait("qs", "cc", cc_done)
    p.wait("t", "v", p.val("v"))
    p.wait("t", "s", p.val("s"))

    HB = 5120

    def kat(b):
        return wst[:, b * HB: b * HB + 2048]

    def vat(b):
        return wst[:, b * HB + 2048: b * HB + 4096]

    def qat(b):
        return wst[:, b * HB + 4096: b * HB + 5120]

    def pat(i):
        return wst[:, 10240 + (i % 3) * 1024: 10240 + (i % 3) * 1024 + 1024]

    def load_head(h, b):
        sem = "hl0" if b == 0 else "hl1"
        p.dma("qs", kat(b)[:, 0:1024], A["k_all"][h * 128:(h + 1) * 128, :], sem)
        p.dma("qs", kat(b)[:, 1024:2048],
              A["k_all"][2048 + h * 128: 2048 + (h + 1) * 128, :], sem)
        p.dma("qs", vat(b).rearrange("q (kb d) -> q kb d", kb=16),
              A["v_all"][:, h * 128:(h + 1) * 128]
              .rearrange("(kb q) d -> q kb d", q=128), sem)
        v = p.dma("qs", qat(b), A["q_dram"][h * 128:(h + 1) * 128, :], sem)
        return sem, v

    p.wait("qs", "t", st["wst0_free"])
    hl = {0: load_head(0, 0)}
    pslot_free = {}
    for h in range(NH):
        b = h % 2
        if h + 1 < NH:
            p.wait("qs", "t", st.get(("kvf", (h + 1) % 2), 0))
            hl[h + 1] = load_head(h + 1, (h + 1) % 2)
        p.wait("t", hl[h][0], hl[h][1])
        _trips = []
        for _s in range(2):
            for _kb in range(8):
                _qs = 0 if _s == 0 else _kb * 128
                _q = _qs
                while _q < TOK:
                    _qe = min(TOK, (_q // 512 + 1) * 512)
                    _trips.append((_s, _kb, _q, _qe))
                    _q = _qe
        last_lo = max(i for i, t in enumerate(_trips) if t[2] < 512)
        last_hi = max(i for i, t in enumerate(_trips) if t[2] >= 512)
        last_av = len(_trips) - 1
        _ti = 0
        sc_i = 0
        for s in range(2):
            for kb in range(8):
                slot = s * 8 + kb
                qstart = 0 if s == 0 else kb * 128
                even = sc_i % 2 == 0
                ktile = kat(b)[:, slot * 128:(slot + 1) * 128]
                pieces = []
                q = qstart
                while q < TOK:
                    qe = min(TOK, (q // 512 + 1) * 512)
                    pieces.append((q, qe))
                    q = qe

                def psc(qa, qe, ev=even):
                    if ev:
                        return pw0[:, qa:qe]
                    return (p0 if qa < 512 else p1)[:, qa - (qa // 512) * 512:
                                                    qe - (qa // 512) * 512]

                p.wait("t", "s", st.get(("scf", sc_i % 2), 0))
                for i, (qa, qe) in enumerate(pieces):
                    p.mm(psc(qa, qe), ktile, qat(b)[:, qa:qe], True, True,
                         sem="t" if i == len(pieces) - 1 else None,
                         ldw=(i == 0))
                t_sc = p.val("t")
                p.wait("s", "t", t_sc)
                p.wait("s", "t", pslot_free.get(sc_i % 3, 0))
                P = pat(sc_i)
                for i, (qa, qe) in enumerate(pieces):
                    p.s("activation", P[:, qa:qe], psc(qa, qe), AF.Exp,
                        bias=cols[:, 0:1], scale=1.0,
                        sem="s" if i == len(pieces) - 1 else None)
                st[("scf", sc_i % 2)] = p.val("s")
                s_exp = p.val("s")
                p.wait("v", "s", s_exp)
                if s == 0:
                    for qb2 in range(kb):
                        p.v("tensor_mul", P[:, qb2 * 128:(qb2 + 1) * 128],
                            P[:, qb2 * 128:(qb2 + 1) * 128], masks[:, 128:256])
                    p.v("tensor_mul", P[:, kb * 128:(kb + 1) * 128],
                        P[:, kb * 128:(kb + 1) * 128], masks[:, 0:128], sem="v")
                else:
                    p.v("tensor_mul", P[:, kb * 128:(kb + 1) * 128],
                        P[:, kb * 128:(kb + 1) * 128], masks[:, 256:384])
                    for qb2 in range(kb + 1, 8):
                        p.v("tensor_mul", P[:, qb2 * 128:(qb2 + 1) * 128],
                            P[:, qb2 * 128:(qb2 + 1) * 128], masks[:, 384:512])
                    p.v("engine_nop", sem="v")
                v_mask = p.val("v")
                p.wait("t", "v", v_mask)
                first_iter = (sc_i == 0)
                for pi, (qa, qe) in enumerate(pieces):
                    lrow = (p2, p3)[qa // 512]
                    is_last = (_ti + pi) == (last_lo if qa < 512 else last_hi)
                    p.mm(lrow[0:1, qa - (qa // 512) * 512: qe - (qa // 512) * 512],
                         B["ones16"][:, :], P[:, qa:qe], first_iter, is_last)
                vt = vat(b)[:, slot * 128:(slot + 1) * 128]
                for i, (qa, qe) in enumerate(pieces):
                    p.mm(pw1[:, qa:qe], vt, P[:, qa:qe], first_iter,
                         (_ti + i) == (last_lo if qa < 512 else last_hi),
                         sem="t" if i == len(pieces) - 1 else None)
                _ti += len(pieces)
                pslot_free[sc_i % 3] = p.val("t")
                sc_i += 1
        t_head = p.val("t")
        st[("kvf", b)] = t_head
        p.wait("v", "t", t_head)
        p.v("reciprocal", rows[0:1, 0:512], p2[0:1, :])
        p.v("reciprocal", rows[0:1, 512:1024], p3[0:1, :], sem="v")
        v_il = p.val("v")
        p.wait("t", "v", v_il)
        p.wait("t", "s", p.val("s"))
        p.mmf(pw0[:, 0:512], B["onesf"][0:1, :], rows[0:1, 0:512], True, True)
        p.mmf(pw0[:, 512:1024], B["onesf"][0:1, :], rows[0:1, 512:1024], True, True)
        _pad(p, B)
        t_il = p.val("t")
        p.wait("s", "t", t_il)
        p.s("copy", B["repb"][:, 0:512], pw0[:, 0:512])
        p.s("copy", B["repb"][:, 512:1024], pw0[:, 512:1024], sem="s")
        s_il = p.val("s")
        p.wait("v", "s", s_il)
        p.v("tensor_mul", anf(h), pw1[:, :], B["repb"][:, 0:1024], sem="v")
        v_o = p.val("v")
        st[("scf", 0)] = max(st.get(("scf", 0), 0), p.val("s"))
        p.wait("t", "v", v_o)

    st[("pof", 0)] = st[("pof", 1)] = p.val("v")
    p.wait("t", "s", p.val("s"))
    for og in range(4):
        evs = _wload_attnT(p, B, A, st, A["cwo"], og,
                           [B["p0"], B["p1"], B["p2"], B["p3"]], p.val("t"))
        for oc in range(4):
            o = og * 4 + oc
            po = (pw0, pw1)[oc % 2]
            p.wait("t", "v", st.get(("pof", oc % 2), 0))
            p.wait("t", "v", evs[oc])
            for hh in range(16):
                lhs = wst[:, oc * 2048 + hh * 128: oc * 2048 + hh * 128 + 128]
                for qh in range(2):
                    p.mm(po[:, qh * 512:(qh + 1) * 512], lhs,
                         anf(hh)[:, qh * 512:(qh + 1) * 512], hh == 0, hh == 15,
                         sem="t" if (hh == 15 and qh == 1) else None,
                         ldw=(qh == 0))
            _pad(p, B)
            t_o = p.val("t")
            p.wait("v", "t", t_o)
            p.v("scalar_tensor_tensor", xu[:, o * TOK:(o + 1) * TOK], po[:, :],
                bsc[:, 48 + o:48 + o + 1], xu[:, o * TOK:(o + 1) * TOK],
                ALU.add, ALU.add, sem="v")
            st[("pof", oc % 2)] = p.val("v")
    p.barrier()


def epilogue(p, B, A, st):
    vy = p.dma("qg", A["y"].rearrange("(j q) t -> q j t", q=128),
               B["xu"][:, :].rearrange("q (j t) -> q j t", j=16), "qgS")
    p.wait("qg", "qgS", vy)
    p.barrier()



def _host_consts():
    h128 = _hadamard(128).astype(ml_dtypes.bfloat16)
    iden = np.eye(128, dtype=np.float32)
    sel4 = np.zeros((4, TH), np.float32)
    for b in range(4):
        sel4[b, b * 128:(b + 1) * 128] = 1.0
    ones = np.ones((128, 128), np.float32)
    return h128, iden, sel4, ones


def _masks_for(parity):
    tri = np.zeros((128, 128), np.float32)
    for i in range(128):
        tri[i, i:] = 1.0
    onesq = np.ones((128, 128), np.float32)
    zer = np.zeros((128, 128), np.float32)
    if parity == 0:
        tiles = [tri, zer, zer, zer]
    else:
        tiles = [onesq, onesq, tri, onesq]
    return np.concatenate(tiles, axis=1).astype(ml_dtypes.bfloat16)


def kernel(L0_attn_X, L0_attn_Wq, L0_attn_bq, L0_attn_Wk, L0_attn_bk,
           L0_attn_Wv, L0_attn_bv, L0_attn_Wo, L0_attn_bo,
           L0_ffn_w_gate, L0_ffn_w_up, L0_ffn_w_down):
    X = np.ascontiguousarray(np.asarray(L0_attn_X, np.float32))
    WQ = np.ascontiguousarray(np.asarray(L0_attn_Wq, np.float32))
    WK = np.ascontiguousarray(np.asarray(L0_attn_Wk, np.float32))
    WV = np.ascontiguousarray(np.asarray(L0_attn_Wv, np.float32))
    WO = np.ascontiguousarray(np.asarray(L0_attn_Wo, np.float32))
    WG = np.ascontiguousarray(np.asarray(L0_ffn_w_gate, np.float32))
    WU = np.ascontiguousarray(np.asarray(L0_ffn_w_up, np.float32))
    WD = np.ascontiguousarray(np.asarray(L0_ffn_w_down, np.float32))
    bq = np.asarray(L0_attn_bq, np.float32)
    bk = np.asarray(L0_attn_bk, np.float32)
    bv = np.asarray(L0_attn_bv, np.float32)
    bo = np.asarray(L0_attn_bo, np.float32)

    h128, iden, sel4, ones = _host_consts()
    idenb16 = np.eye(128).astype(ml_dtypes.bfloat16)
    bcols = np.zeros((128, 64), np.float32)
    bcols[:, 0:16] = bq.reshape(16, 128).T
    bcols[:, 16:32] = bk.reshape(16, 128).T
    bcols[:, 32:48] = bv.reshape(16, 128).T
    bcols[:, 48:64] = bo.reshape(16, 128).T
    bvrow = bv.reshape(1, D)

    if "nc" not in _CACHE:
        _CACHE["nc"] = build_program()
    nc = _CACHE["nc"]

    in_maps = []
    for c in range(8):
        seq, half = c // 2, c % 2
        xs = np.ascontiguousarray(X[seq, half * TOK:(half + 1) * TOK, :].T)
        wsh = np.concatenate([
            WG[c * 1024:(c + 1) * 1024, :],
            WU[c * 1024:(c + 1) * 1024, :],
            WD[c * 256:(c + 1) * 256, :].reshape(1024, 2048),
        ], axis=0)
        in_maps.append({
            "x0": xs, "wq": WQ, "wk": WK, "wv": WV, "wo": WO,
            "wg": WG, "wu": WU, "wdn": WD,
            "wshard": np.ascontiguousarray(wsh),
            "bcols": bcols, "bvrow": bvrow,
            "h128c": h128, "idenbc": idenb16, "idenc": iden, "sel4c": sel4,
            "onesc": ones,
            "m01": _masks_for(half),
        })

    res = run_bass_kernel_spmd(nc, in_maps, core_ids=list(range(8)))
    out = np.empty((4, 2048, 2048), np.float32)
    for c in range(8):
        seq, half = c // 2, c % 2
        out[seq, half * TOK:(half + 1) * TOK, :] = res.results[c]["y"].T
    if _DBG.get("keep"):
        _DBG["res"] = res
    return out
